# version 17
# baseline (speedup 1.0000x reference)
import numpy as np
from contextlib import ExitStack

import concourse.bass as bass
import concourse.mybir as mybir
from concourse.bass_utils import run_bass_kernel_spmd

F32 = mybir.dt.float32
AF = mybir.ActivationFunctionType
ALU = mybir.AluOpType
AXX = mybir.AxisListType.X

ENGS = ("tensor", "vector", "scalar", "sync", "gpsimd")

D = 1024
H = 16
DK = 64
NE = 8
SI = 2048
MSHIFT = -8.0
SCALE = 0.125
NEGBIG = -8.0e9
NONE_TOK = (None, 0)


class _Sem:
    __slots__ = ("h", "n", "name")


class Prog:

    def __init__(self, nc, es):
        self.nc = nc
        self.es = es
        self.ops = {e: [] for e in ENGS}
        self.qsem = {}

    def sem(self, name):
        s = _Sem()
        s.h = self.es.enter_context(self.nc.semaphore(name))
        s.n = 0
        s.name = name
        return s

    def sb(self, name, shape, dtype=F32):
        return self.es.enter_context(self.nc.sbuf_tensor(name, shape, dtype))

    def ps(self, name, shape, dtype=F32):
        return self.es.enter_context(self.nc.psum_tensor(name, shape, dtype))

    def op(self, eng, fn, waits=(), inc=None, n=1):
        waits = tuple((s, v) for (s, v) in waits if s is not None and v > 0)
        self.ops[eng].append((waits, fn, inc, n))
        if inc is not None:
            inc.n += n
            return (inc, inc.n)
        return None

    def dma(self, eng, out, in_, waits=(), **kw):
        sems = self.qsem[eng]
        i = self.qrr[eng]
        self.qrr[eng] = (i + 1) % len(sems)
        sem = sems[i]
        waits = list(waits) + [(sem, sem.n)]
        return self.op(eng, lambda E: E.dma_start(out=out, in_=in_, **kw), waits, sem, 16)

    def tok(self, sem):
        return (sem, sem.n)

    def build(self):
        with self.nc.Block() as blk:
            for eng in ENGS:
                getattr(blk, eng)(self._replayer(eng))

    def _replayer(self, eng):
        entries = self.ops[eng]

        def run(E):
            water = {}
            for waits, fn, inc, n in entries:
                pend = []
                for (s, v) in waits:
                    if water.get(id(s), 0) < v:
                        pend.append((s, v))
                        water[id(s)] = v
                attach = False
                for (s, v) in (pend[:-1] if attach else pend):
                    E.wait_ge(s.h, v)
                inst = fn(E)
                if inst is None:
                    if attach:
                        E.wait_ge(pend[-1][0].h, pend[-1][1])
                else:
                    if attach:
                        inst._wait_ge(pend[-1][0].h, pend[-1][1])
                    if inc is not None:
                        inst.then_inc(inc.h, n)
            return None

        return run


class Builder:
    def __init__(self, nc, es, T):
        self.nc = nc
        self.es = es
        self.T = T
        self.TT = T // 128
        self.KC = T // 128
        self.QGS = min(512, T)
        self.QG = T // self.QGS
        self.DQG = self.QGS // 128
        p = self.P = Prog(nc, es)

        self.mm = p.sem("mm")
        self.dv = p.sem("dv")
        self.ac = p.sem("ac")
        p.qsem["sync"] = [p.sem(f"qs{i}") for i in range(4)]
        p.qsem["gpsimd"] = [p.sem(f"qg{i}") for i in range(4)]
        p.qrr = {"sync": 0, "gpsimd": 0}

        dt = nc.dram_tensor
        self.x_in = dt("x_in", [T, D], F32, kind="ExternalInput").ap()
        self.y_out = dt("y_out", [T, D], F32, kind="ExternalOutput").ap()
        self.wqkv_a = dt("wqkv_a", [8, 128, 3072], F32, kind="ExternalInput").ap()
        self.wo_a = dt("wo_a", [D, D], F32, kind="ExternalInput").ap()
        self.wqkv_b = dt("wqkv_b", [8, 128, 3072], F32, kind="ExternalInput").ap()
        self.wo_b = dt("wo_b", [D, D], F32, kind="ExternalInput").ap()
        self.cosx = dt("cosx", [T, 128], F32, kind="ExternalInput").ap()
        self.sinx = dt("sinx", [T, 128], F32, kind="ExternalInput").ap()
        self.maskd = dt("maskd", [128, self.DQG * self.QGS], F32, kind="ExternalInput").ap()
        self.maskqmd = dt("maskqmd", [128, 128], F32, kind="ExternalInput").ap()
        self.identd = dt("identd", [128, 128], F32, kind="ExternalInput").ap()
        self.gwd = dt("gwd", [128, 64], F32, kind="ExternalInput").ap()
        self.gbiasd = dt("gbiasd", [128, NE], F32, kind="ExternalInput").ap()
        self.w13d = dt("w13d", [NE, 8, 128, 2048], F32, kind="ExternalInput").ap()
        self.w2d = dt("w2d", [NE, D, D], F32, kind="ExternalInput").ap()
        self.sw13d = dt("sw13d", [16, 128, 2048], F32, kind="ExternalInput").ap()
        self.sw2d = dt("sw2d", [SI, D], F32, kind="ExternalInput").ap()
        self.debug = False
        self.dbg = {}

        sb = p.sb
        self.Ah = sb("A_xnTh", [128, 8 * T], mybir.dt.bfloat16)
        self.Al = sb("A_xnTl", [128, 8 * T], mybir.dt.bfloat16)
        self.B = sb("B_x", [128, self.TT * D])
        self.C = sb("C_arena", [128, 15936])
        self.ident = sb("ident", [128, 128])
        self.gbias = sb("gbias", [128, NE])
        self.GS = sb("GS", [128, self.TT * NE])
        self.stats = sb("stats", [128, 32])
        self.xn_s = sb("xn_s", [128, D])
        self.rope_s = sb("rope_s", [128, 384])
        self.ones1 = sb("ones1", [1, 128])
        self.r_sb = sb("r_sb", [1, 1536])
        self.gsc = sb("gsc", [128, 40])
        self.gwh = sb("gwh", [128, 64], mybir.dt.bfloat16)
        self.gwl = sb("gwl", [128, 64], mybir.dt.bfloat16)
        self.ck = sb("ck", [128, 2])
        self.otr = self.xn_s[0:64, 0:512]
        self.maskqm = sb("maskqm", [128, 128])
        self.ones_bf = sb("ones_bf", [1, 128], mybir.dt.bfloat16)
        self.msb_bf = sb("msb_bf", [1, 1024], mybir.dt.bfloat16)

        self.pb = [p.ps(f"pb{i}", [128, 512]) for i in range(8)]

        a = {}
        o = 0
        for name, n in (("QTh", T // 2), ("QTl", T // 2), ("KTh", T // 2), ("KTl", T // 2),
                        ("Vh", self.KC * 66), ("Vl", self.KC * 66),
                        ("ET", 2 * self.QGS), ("OT0", self.QGS), ("OT1", self.QGS),
                        ("wqkv", 8 * 384), ("wo0", D), ("wo1", D),
                        ("cs", 256), ("mask", self.DQG * self.QGS), ("qr", 256)):
            a[name] = o
            o += n
        assert o <= 15936, o
        self.at = a
        f = {}
        o = 0
        for name, n in (("hT", 16 * 512), ("w13", 6 * 1024), ("w2", 3 * 512), ("tsb", 0)):
            f[name] = o
            o += n
        assert o <= 15936, o
        self.ft = f

    def dump(self, name, src_ap, shape):
        if not self.debug:
            return
        p = self.P
        d = self.nc.dram_tensor("dbg_" + name, shape, F32, kind="ExternalOutput").ap()
        toks = [p.tok(self.mm), p.tok(self.dv), p.tok(self.ac)]
        t = p.dma("sync", d, src_ap, waits=toks)
        p.op("sync", lambda E: None, waits=[t])

    def dump_B(self, name):
        if not self.debug:
            return
        p = self.P
        d = self.nc.dram_tensor("dbg_" + name, [self.T, D], F32, kind="ExternalOutput").ap()
        toks = [p.tok(self.mm), p.tok(self.dv), p.tok(self.ac)]
        t = p.dma("sync", d.rearrange("(t p) d -> p t d", p=128),
                  self.B.rearrange("p (t d) -> p t d", d=D), waits=toks)
        p.op("sync", lambda E: None, waits=[t])
        self.tok_B = p.op("vector", lambda E: E.engine_nop(), waits=[t], inc=self.dv)

    def dump_A(self, name):
        if not self.debug:
            return
        p = self.P
        d = self.nc.dram_tensor("dbg_" + name, [8, 128, self.T], F32, kind="ExternalOutput").ap()
        toks = [p.tok(self.mm), p.tok(self.dv), p.tok(self.ac)]
        t = p.dma("gpsimd", d.rearrange("c p t -> p c t"),
                  self.Ah.rearrange("p (c t) -> p c t", t=self.T), waits=toks)
        p.op("gpsimd", lambda E: None, waits=[t])

    def fence(self):
        p = self.P
        toks = [p.tok(self.mm), p.tok(self.dv), p.tok(self.ac)]
        for eng in ("tensor", "vector", "scalar"):
            p.op(eng, lambda E: None, waits=toks)
        return p.tok(self.mm)

    def ln_stage(self, extra=()):
        p = self.P
        mm, dv, ac = self.mm, self.dv, self.ac
        st = self.stats
        prev_tp = {}
        tp_done = {0: NONE_TOK, 1: NONE_TOK}
        xn_tok = {}
        for tti in range(self.TT + 1):
            if tti < self.TT:
                tt = tti
                par = tt % 2
                cb = 0 if par == 0 else 16
                xt = self.B[:, tt * D:(tt + 1) * D]
                xnb = self.xn_s if par == 0 else self.C[:, 1024:2048]
                dummy = self.C[:, 0:1024]
                w = list(extra)
                t_s1 = p.op("vector", lambda E, xt=xt, cb=cb: E.reduce_sum(st[:, cb:cb + 1], xt, axis=AXX), w, dv)
                t_s2 = p.op("scalar", lambda E, xt=xt, cb=cb, dummy=dummy:
                            E.activation(dummy, xt, AF.Square, accum_out=st[:, cb + 1:cb + 2]),
                            w + [self.ln_dummy_free], ac)
                self.ln_dummy_free = t_s2
                t_m = p.op("vector", lambda E, cb=cb: E.tensor_scalar(st[:, cb + 2:cb + 3], st[:, cb:cb + 1], 1.0 / D, None, ALU.mult), [t_s1], dv)
                t_mm2 = p.op("vector", lambda E, cb=cb: E.tensor_mul(st[:, cb + 3:cb + 4], st[:, cb + 2:cb + 3], st[:, cb + 2:cb + 3]), [t_m], dv)
                t_v = p.op("vector", lambda E, cb=cb: E.scalar_tensor_tensor(st[:, cb + 4:cb + 5], st[:, cb + 1:cb + 2], 1.0 / D, st[:, cb + 3:cb + 4],
                                                                            ALU.mult, ALU.subtract), [t_s2, t_mm2], dv)
                t_ve = p.op("vector", lambda E, cb=cb: E.tensor_scalar(st[:, cb + 5:cb + 6], st[:, cb + 4:cb + 5], 1e-5, None, ALU.add), [t_v], dv)
                t_sq = p.op("scalar", lambda E, cb=cb: E.activation(st[:, cb + 6:cb + 7], st[:, cb + 5:cb + 6], AF.Sqrt), [t_ve], ac)
                tk = p.op("vector", lambda E, cb=cb: E.reciprocal(st[:, cb + 7:cb + 8], st[:, cb + 6:cb + 7]), [t_sq], dv)
                for _ in range(2):
                    t_a = p.op("vector", lambda E, cb=cb: E.tensor_mul(st[:, cb + 8:cb + 9], st[:, cb + 7:cb + 8], st[:, cb + 7:cb + 8]), [tk], dv)
                    t_b = p.op("vector", lambda E, cb=cb: E.tensor_mul(st[:, cb + 9:cb + 10], st[:, cb + 5:cb + 6], st[:, cb + 8:cb + 9]), [t_a], dv)
                    t_c = p.op("vector", lambda E, cb=cb: E.tensor_scalar(st[:, cb + 9:cb + 10], st[:, cb + 9:cb + 10], -0.5, 1.5, ALU.mult, ALU.add), [t_b], dv)
                    tk = p.op("vector", lambda E, cb=cb: E.tensor_mul(st[:, cb + 7:cb + 8], st[:, cb + 7:cb + 8], st[:, cb + 9:cb + 10]), [t_c], dv)
                t_xn = p.op("vector", lambda E, xt=xt, cb=cb, xnb=xnb: E.tensor_scalar(xnb[:, 0:D], xt, st[:, cb + 2:cb + 3], st[:, cb + 7:cb + 8],
                                                                                     ALU.subtract, ALU.mult), [tk, t_s2, tp_done[par]], dv)
                xn_tok[tt] = t_xn
            if tti >= 1:
                tt = tti - 1
                par = tt % 2
                xnb = self.xn_s if par == 0 else self.C[:, 1024:2048]
                t_xn = xn_tok.pop(tt)
                last = NONE_TOK
                for fc in range(8):
                    sl = fc % 2
                    pslot = self.pb[6 + sl][:, 0:128]
                    dsth = self.Ah[:, fc * self.T + tt * 128: fc * self.T + tt * 128 + 128]
                    dstl = self.Al[:, fc * self.T + tt * 128: fc * self.T + tt * 128 + 128]
                    wt = [t_xn, prev_tp.get(sl, NONE_TOK)]
                    t_tp = p.op("tensor", lambda E, pslot=pslot, fc=fc, xnb=xnb:
                                E.transpose(pslot, xnb[:, fc * 128:(fc + 1) * 128], self.ident[:]), wt, mm)
                    t_evh = p.op("scalar", lambda E, pslot=pslot, dsth=dsth: E.copy(dsth, pslot), [t_tp], ac)
                    t_evl = p.op("vector", lambda E, pslot=pslot, dsth=dsth, dstl=dstl:
                                 E.tensor_sub(dstl, pslot, dsth), [t_evh], dv)
                    prev_tp[sl] = t_evl
                    last = t_tp
                tp_done[par] = last
        self.tok_xnT = [p.tok(self.ac), p.tok(self.dv)]

    def attn_stage(self, wqkv_d, wo_d, use_rope, causal, arena_gate):
        p = self.P
        mm, dv, ac = self.mm, self.dv, self.ac
        a, C, T = self.at, self.C, self.T
        TT, KC, QGS, QG, DQG = self.TT, self.KC, self.QGS, self.QG, self.DQG
        BF16 = mybir.dt.bfloat16
        xnT_tok = self.tok_xnT

        QTh = C[:, a["QTh"]:a["QTh"] + T // 2].bitcast(BF16)
        QTl = C[:, a["QTl"]:a["QTl"] + T // 2].bitcast(BF16)
        KTh = C[:, a["KTh"]:a["KTh"] + T // 2].bitcast(BF16)
        KTl = C[:, a["KTl"]:a["KTl"] + T // 2].bitcast(BF16)
        Vh = C[:, a["Vh"]:a["Vh"] + KC * 66].bitcast(BF16).rearrange("p (k c) -> p k c", c=132)
        Vl = C[:, a["Vl"]:a["Vl"] + KC * 66].bitcast(BF16).rearrange("p (k c) -> p k c", c=132)
        wqh = C[:, a["wqkv"]:a["wqkv"] + 1536].bitcast(mybir.dt.bfloat16)
        wql = C[:, a["wqkv"] + 1536:a["wqkv"] + 3072].bitcast(mybir.dt.bfloat16)
        woh = [C[0:64, a["wo0"]:a["wo0"] + 512].bitcast(BF16),
               C[0:64, a["wo1"]:a["wo1"] + 512].bitcast(BF16)]
        wol = [C[0:64, a["wo0"] + 512:a["wo0"] + 1024].bitcast(BF16),
               C[0:64, a["wo1"] + 512:a["wo1"] + 1024].bitcast(BF16)]
        masks = C[:, a["mask"]:a["mask"] + DQG * QGS]
        qr = [C[:, a["qr"]:a["qr"] + 128], C[:, a["qr"] + 128:a["qr"] + 256]]

        if causal:
            t_mask = p.dma("sync", masks, self.maskd[:], waits=[arena_gate])
        else:
            t_mask = NONE_TOK
        t_ones0 = p.op("vector", lambda E: E.memset(Vh[:, :, 64:65], 1.0), [arena_gate] + xnT_tok, dv)
        t_ones1 = p.op("vector", lambda E: E.memset(Vh[:, :, 130:131], 1.0), [], dv)
        t_ones2 = p.op("vector", lambda E: E.memset(Vl[:, :, 64:65], 0.0), [], dv)
        t_ones3 = p.op("vector", lambda E: E.memset(Vl[:, :, 130:131], 0.0), [], dv)
        t_onesV = [t_ones0, t_ones1, t_ones2, t_ones3]

        prev_hp_done = NONE_TOK
        prev_hp_dv = NONE_TOK
        prev_wo_ev = NONE_TOK
        pb2_free = NONE_TOK
        et_free = [NONE_TOK] * 6
        exp_done = {}
        tp_free = {}
        epi = {}
        qr_free = NONE_TOK
        ot_evac = {}
        R_free = self.R_free_init
        woev = {}
        self.cs_free_tok = arena_gate
        self.otr_free = NONE_TOK
        self.msb_free = {}

        for hp in range(8):
            wq_src = wqkv_d[hp]
            t_wqh = p.dma("gpsimd", wqh[:], wq_src,
                          waits=[arena_gate if hp == 0 else prev_hp_done])
            t_wqn = p.op("vector", lambda E: E.tensor_scalar(wql[:], wqh[:], -1.0, None, ALU.mult),
                         [t_wqh, prev_hp_done if hp else arena_gate], dv)
            rows0 = wo_d[hp * 128: hp * 128 + 64, :]
            rows1 = wo_d[hp * 128 + 64: hp * 128 + 128, :]
            t_h0 = p.dma("gpsimd", woh[0], rows0, waits=[prev_wo_ev])
            t_n0 = p.op("vector", lambda E: E.tensor_scalar(wol[0], woh[0], -1.0, None, ALU.mult),
                        [t_h0, prev_wo_ev], dv)
            t_h1 = p.dma("gpsimd", woh[1], rows1, waits=[prev_wo_ev])
            t_n1 = p.op("vector", lambda E: E.tensor_scalar(wol[1], woh[1], -1.0, None, ALU.mult),
                        [t_h1, prev_wo_ev], dv)
            t_wqla = p.dma("gpsimd", wql[:, 0:1536], wq_src[:, 0:1536],
                           waits=[t_wqn], accum_op=ALU.add)
            t_wqlb = p.dma("gpsimd", wql[:, 1536:3072], wq_src[:, 1536:3072],
                           waits=[t_wqn], accum_op=ALU.add)
            t_l0 = p.dma("gpsimd", wol[0], rows0, waits=[t_n0], accum_op=ALU.add)
            t_l1 = p.dma("gpsimd", wol[1], rows1, waits=[t_n1], accum_op=ALU.add)
            t_wq = [t_wqh, t_wqla, t_wqlb]
            t_wo = [t_h0, t_l0, t_h1, t_l1]

            for tt in range(TT):
                pbq = self.pb[tt % 2]
                waits = list(xnT_tok) + t_wq + epi.get(tt - 2, [prev_hp_done, prev_hp_dv])
                tk = None
                for k in range(8):
                    xh = self.Ah[:, k * T + tt * 128:k * T + tt * 128 + 128]
                    xl = self.Al[:, k * T + tt * 128:k * T + tt * 128 + 128]
                    rh = wqh[:, k * 384:(k + 1) * 384]
                    rl = wql[:, k * 384:(k + 1) * 384]
                    p.op("tensor", lambda E, pbq=pbq, xh=xh, rh=rh, k=k:
                         E.matmul(pbq[:, 0:384], xh, rh, start=(k == 0), stop=False),
                         waits if k == 0 else [], mm)
                    p.op("tensor", lambda E, pbq=pbq, xh=xh, rl=rl:
                         E.matmul(pbq[:, 0:384], xh, rl, start=False, stop=False), [], mm)
                    tk = p.op("tensor", lambda E, pbq=pbq, xl=xl, rh=rh, k=k:
                              E.matmul(pbq[:, 0:384], xl, rh, start=False, stop=(k == 7)), [], mm)
                deps = [tk]
                cs_aps = (None, None)
                if use_rope:
                    csl = C[:, a["cs"]:a["cs"] + 128]
                    snl = C[:, a["cs"] + 128:a["cs"] + 256]
                    t_c = p.dma("sync", csl, self.cosx[tt * 128:(tt + 1) * 128, :], waits=[self.cs_free_tok])
                    t_s = p.dma("sync", snl, self.sinx[tt * 128:(tt + 1) * 128, :], waits=[self.cs_free_tok])
                    deps += [t_c, t_s]
                    cs_aps = (csl, snl)
                toks = self._qkv_epilogue(pbq, tt, use_rope, deps, cs_aps, qr, qr_free,
                                          tp_free, pb2_free, (QTh, QTl, KTh, KTl), (Vh, Vl))
                epi[tt] = toks["free_psum"]
                qr_free = toks["qr_free"]
                if use_rope:
                    self.cs_free_tok = toks["cs_free"]
            qt_ready = [p.tok(self.ac), p.tok(self.dv)]

            for qg in range(QG):
                ot_toks = {}
                kc_hi = (qg + 1) * DQG if causal else KC
                mtoks_hl = {}
                for hl in range(2):
                    mtoks = []
                    Msb = self.msb_bf[0:1, hl * 512:hl * 512 + QGS]
                    for ti in range(DQG):
                        qt = qg * DQG + ti
                        hi_k = (qt + 1) * 128 if causal else T
                        kgs = (hi_k + 511) // 512
                        mcol = self.stats[:, 12 + hl * 8 + ti:13 + hl * 8 + ti]
                        tcol = self.stats[:, 16 + hl * 8 + ti:17 + hl * 8 + ti]
                        t_red = NONE_TOK
                        for kg in range(kgs):
                            n_v = min(512, hi_k - kg * 512)
                            par = kg % 2
                            pbA = self.pb[3 + par]
                            qsl = QTh[hl * 64:(hl + 1) * 64, qt * 128:(qt + 1) * 128]
                            ksl = KTh[hl * 64:(hl + 1) * 64, kg * 512:kg * 512 + n_v]
                            t_a = p.op("tensor", lambda E, pbA=pbA, qsl=qsl, ksl=ksl, n_v=n_v:
                                       E.matmul(pbA[:, 0:n_v], qsl, ksl, start=True, stop=True),
                                       qt_ready + [exp_done.get(par, NONE_TOK)], mm)
                            if causal and kg == kgs - 1:
                                off = qt * 128 - kg * 512
                                t_a = p.op("vector", lambda E, pbA=pbA, off=off:
                                           E.tensor_add(pbA[:, off:off + 128], pbA[:, off:off + 128], self.maskqm[:]),
                                           [t_a, self.t_maskqm], dv)
                            if kg == 0:
                                t_red = p.op("vector", lambda E, pbA=pbA, n_v=n_v, mcol=mcol:
                                             E.reduce_max(mcol, pbA[:, 0:n_v], axis=AXX),
                                             [t_a, mtoks[-1] if mtoks else NONE_TOK], dv)
                            else:
                                t_r1 = p.op("vector", lambda E, pbA=pbA, n_v=n_v, tcol=tcol:
                                            E.reduce_max(tcol, pbA[:, 0:n_v], axis=AXX), [t_a], dv)
                                t_red = p.op("vector", lambda E, mcol=mcol, tcol=tcol:
                                             E.tensor_max(mcol, mcol, tcol), [t_r1, t_red], dv)
                            exp_done[par] = t_red
                        t_tp = p.op("tensor", lambda E, mcol=mcol: E.transpose(self.pb[6][0:1, 0:128], mcol, self.ident[:]),
                                    [t_red, R_free], mm)
                        t_ng = p.op("vector", lambda E, Msb=Msb, ti=ti: E.tensor_scalar(
                                        Msb[0:1, ti * 128:ti * 128 + 128], self.pb[6][0:1, 0:128], -1.0, None, ALU.mult),
                                    [t_tp, self.msb_free.get(hl, NONE_TOK)], dv)
                        mtoks.append(t_ng)
                    mtoks_hl[hl] = mtoks
                pv_tok = {}
                for hl in range(2):
                    pbO = self.pb[5] if hl == 0 else self.pb[7]
                    mtoks = mtoks_hl[hl]
                    Msb = self.msb_bf[0:1, hl * 512:hl * 512 + QGS]
                    ex_tok = {}
                    for kc in range(kc_hi + 1):
                        if kc < kc_hi:
                            ssl = kc % 2
                            pbS = self.pb[3 + ssl]
                            kth = KTh[hl * 64:(hl + 1) * 64, kc * 128:(kc + 1) * 128]
                            ktl = KTl[hl * 64:(hl + 1) * 64, kc * 128:(kc + 1) * 128]
                            qth = QTh[hl * 64:(hl + 1) * 64, qg * QGS:(qg + 1) * QGS]
                            qtl = QTl[hl * 64:(hl + 1) * 64, qg * QGS:(qg + 1) * QGS]
                            w_s = qt_ready + [exp_done.get(ssl, NONE_TOK)]
                            p.op("tensor", lambda E, pbS=pbS, kth=kth, qth=qth:
                                 E.matmul(pbS[:, 0:QGS], kth, qth, start=True, stop=False), w_s, mm)
                            p.op("tensor", lambda E, pbS=pbS, kth=kth, qtl=qtl:
                                 E.matmul(pbS[:, 0:QGS], kth, qtl, start=False, stop=False), [], mm)
                            p.op("tensor", lambda E, pbS=pbS, ktl=ktl, qth=qth:
                                 E.matmul(pbS[:, 0:QGS], ktl, qth, start=False, stop=False), [], mm)
                            t_sc = p.op("tensor", lambda E, pbS=pbS, Msb=Msb:
                                        E.matmul(pbS[:, 0:QGS], self.ones_bf[:], Msb, start=False, stop=True),
                                        mtoks, mm)
                            self.msb_free[hl] = t_sc
                            if causal and kc >= qg * DQG:
                                j = kc - qg * DQG
                                t_sc = p.op("vector", lambda E, pbS=pbS, j=j:
                                            E.tensor_add(pbS[:, 0:QGS], pbS[:, 0:QGS], masks[:, j * QGS:(j + 1) * QGS]),
                                            [t_sc, t_mask], dv)
                            n_et = 2 if causal else 6
                            sl = kc % n_et
                            pbE = self.pb[sl % 2]
                            eoff = (a["ET"] + sl * QGS) if sl < 2 else (a["mask"] + (sl - 2) * QGS)
                            eth = C[:, eoff: eoff + QGS // 2].bitcast(BF16)[:, 0:QGS]
                            etl = C[:, eoff + QGS // 2: eoff + QGS].bitcast(BF16)[:, 0:QGS]
                            t_e0 = p.op("scalar", lambda E, pbE=pbE, pbS=pbS:
                                        E.activation(pbE[:, 0:QGS], pbS[:, 0:QGS], AF.Exp, scale=SCALE),
                                        [t_sc, et_free[sl]], ac)
                            t_e1 = p.op("vector", lambda E, eth=eth, pbE=pbE:
                                        E.tensor_copy(eth, pbE[:, 0:QGS]), [t_e0], dv)
                            t_e2 = p.op("vector", lambda E, etl=etl, pbE=pbE, eth=eth:
                                        E.tensor_sub(etl, pbE[:, 0:QGS], eth), [t_e1], dv)
                            exp_done[ssl] = t_e2
                            ex_tok[kc] = (t_e1, t_e2, eth, etl)
                        if kc >= 1:
                            kp = kc - 1
                            t_eh, t_el, eth, etl = ex_tok.pop(kp)
                            vvh = Vh[:, kp, 66 * hl:66 * hl + 65]
                            vvl = Vl[:, kp, 66 * hl:66 * hl + 65]
                            w_pv1 = [t_eh] + t_onesV
                            if kp == 0:
                                w_pv1.append(ot_evac.get(hl, NONE_TOK))
                                if hl == 1:
                                    w_pv1.append(woev.get(0, NONE_TOK))
                            p.op("tensor", lambda E, pbO=pbO, vvh=vvh, eth=eth, s=(kp == 0):
                                 E.matmul(pbO[0:65, 0:QGS], vvh, eth, start=s, stop=False), w_pv1, mm)
                            p.op("tensor", lambda E, pbO=pbO, vvl=vvl, eth=eth:
                                 E.matmul(pbO[0:65, 0:QGS], vvl, eth, start=False, stop=False), [], mm)
                            t_pv = p.op("tensor", lambda E, pbO=pbO, vvh=vvh, etl=etl, e=(kp == kc_hi - 1):
                                        E.matmul(pbO[0:65, 0:QGS], vvh, etl, start=False, stop=e), [t_el], mm)
                            et_free[kp % (2 if causal else 6)] = t_pv
                    pv_tok[hl] = t_pv
                for hl in range(2):
                    pbO = self.pb[5] if hl == 0 else self.pb[7]
                    t_pv = pv_tok[hl]
                    rrow = self.r_sb[0:1, hl * 512:hl * 512 + QGS]
                    sc2 = self.r_sb[0:1, 1024:1024 + QGS]
                    t_r = p.op("vector", lambda E, rrow=rrow, pbO=pbO: E.reciprocal(rrow, pbO[64:65, 0:QGS]), [t_pv], dv)
                    t_t = p.op("vector", lambda E, sc2=sc2, rrow=rrow, pbO=pbO:
                               E.tensor_mul(sc2, pbO[64:65, 0:QGS], rrow), [t_r], dv)
                    t_u = p.op("vector", lambda E, sc2=sc2: E.tensor_scalar(sc2, sc2, -1.0, 2.0, ALU.mult, ALU.add), [t_t], dv)
                    t_rn = p.op("vector", lambda E, rrow=rrow, sc2=sc2: E.tensor_mul(rrow, rrow, sc2), [t_u], dv)
                    pbR = self.pb[6]
                    t_R = p.op("tensor", lambda E, pbR=pbR, rrow=rrow:
                               E.matmul(pbR[0:64, 0:QGS], self.ones1[0:1, 0:64], rrow, start=True, stop=True),
                               [t_rn, R_free], mm)
                    kk = "OT0" if hl == 0 else "OT1"
                    OTh = C[0:64, a[kk]:a[kk] + QGS // 2].bitcast(BF16)[:, 0:QGS]
                    OTl = C[0:64, a[kk] + QGS // 2:a[kk] + QGS].bitcast(BF16)[:, 0:QGS]
                    otr = self.xn_s[0:64, 0:512]
                    otn = self.xn_s[0:64, 512:1024]
                    t_cp = p.op("scalar", lambda E, pbO=pbO, otr=otr: E.copy(otr[:, 0:QGS], pbO[0:64, 0:QGS]),
                                [t_pv, self.otr_free], ac)
                    t_nm = p.op("vector", lambda E, otn=otn, otr=otr, pbR=pbR:
                                E.tensor_mul(otn[:, 0:QGS], otr[:, 0:QGS], pbR[0:64, 0:QGS]),
                                [t_R, t_cp], dv)
                    t_oh = p.op("scalar", lambda E, OTh=OTh, otn=otn: E.copy(OTh, otn[:, 0:QGS]),
                                [t_nm, ot_toks.get(("wo", hl), woev.get("last", NONE_TOK))], ac)
                    t_ol = p.op("vector", lambda E, OTl=OTl, otn=otn, OTh=OTh: E.tensor_sub(OTl, otn[:, 0:QGS], OTh),
                                [t_oh], dv)
                    self.otr_free = t_ol
                    R_free = t_ol
                    ot_evac[hl] = t_ol
                    ot_toks[hl] = [t_oh, t_ol]
                for ti in range(DQG):
                    tt_g = qg * DQG + ti
                    for half in range(2):
                        par = (ti * 2 + half) % 2
                        pbW = self.pb[7] if par == 0 else self.pb[2]
                        w0 = ot_toks[0] + ot_toks[1] + t_wo + [woev.get(par, NONE_TOK)]
                        t_w2 = None
                        for hl in range(2):
                            kk = "OT0" if hl == 0 else "OT1"
                            oh = C[0:64, a[kk]:a[kk] + QGS // 2].bitcast(BF16)[:, ti * 128:ti * 128 + 128]
                            ol = C[0:64, a[kk] + QGS // 2:a[kk] + QGS].bitcast(BF16)[:, ti * 128:ti * 128 + 128]
                            wh = woh[hl][:, half * 512:(half + 1) * 512]
                            wl = wol[hl][:, half * 512:(half + 1) * 512]
                            p.op("tensor", lambda E, pbW=pbW, oh=oh, wh=wh, s=(hl == 0):
                                 E.matmul(pbW[:, 0:512], oh, wh, start=s, stop=False), w0 if hl == 0 else [], mm)
                            p.op("tensor", lambda E, pbW=pbW, oh=oh, wl=wl:
                                 E.matmul(pbW[:, 0:512], oh, wl, start=False, stop=False), [], mm)
                            t_w2 = p.op("tensor", lambda E, pbW=pbW, ol=ol, wh=wh, e=(hl == 1):
                                        E.matmul(pbW[:, 0:512], ol, wh, start=False, stop=e), [], mm)
                        Bsl = self.B[:, tt_g * D + half * 512: tt_g * D + half * 512 + 512]
                        t_ev = p.op("vector", lambda E, Bsl=Bsl, pbW=pbW:
                                    E.tensor_add(Bsl, Bsl, pbW[:, 0:512]), [t_w2, self.tok_B], dv)
                        woev[par] = t_ev
                        woev["last"] = t_ev
                        prev_wo_ev = t_ev
                        if par == 1:
                            pb2_free = t_ev
                ot_toks[("wo", 0)] = woev["last"]
                ot_toks[("wo", 1)] = woev["last"]
            prev_hp_done = p.tok(mm)
            prev_hp_dv = p.tok(dv)
        self.tok_B = p.tok(dv)
        return p.tok(mm)

    def _qkv_epilogue(self, pbq, tt, use_rope, deps, cs_aps, qr, qr_free, tp_free, pb2_free, qk_t, v_t):
        p = self.P
        mm, dv, ac = self.mm, self.dv, self.ac
        QTh, QTl, KTh, KTl = qk_t
        Vh, Vl = v_t
        cs_t, sn_t = cs_aps
        rs = self.rope_s
        out = {}
        last_dv = NONE_TOK
        for idx, (src_off, nm) in enumerate(((0, "q"), (128, "k"))):
            dst = qr[idx]
            ro = idx * 192
            if use_rope:
                qe = pbq[:, src_off:src_off + 128][:, 0::2]
                qo = pbq[:, src_off:src_off + 128][:, 1::2]
                ce = cs_t[:, 0::2]
                co = cs_t[:, 1::2]
                se = sn_t[:, 0::2]
                so = sn_t[:, 1::2]
                de = dst[:, 0::2]
                do = dst[:, 1::2]
                w0 = deps + [qr_free]
                t1 = p.op("vector", lambda E, de=de, qe=qe, ce=ce: E.tensor_mul(de, qe, ce), w0, dv)
                t2 = p.op("vector", lambda E, qo=qo, se=se, ro=ro: E.tensor_mul(rs[:, ro:ro + 64], qo, se), [], dv)
                t3 = p.op("vector", lambda E, de=de, ro=ro: E.tensor_sub(de, de, rs[:, ro:ro + 64]), [t1, t2], dv)
                t4 = p.op("vector", lambda E, qe=qe, so=so, ro=ro: E.tensor_mul(rs[:, ro + 64:ro + 128], qe, so), [], dv)
                t5 = p.op("vector", lambda E, qo=qo, co=co, ro=ro: E.tensor_mul(rs[:, ro + 128:ro + 192], qo, co), [], dv)
                t6 = p.op("vector", lambda E, do=do, ro=ro:
                          E.tensor_add(do, rs[:, ro + 64:ro + 128], rs[:, ro + 128:ro + 192]), [t4, t5], dv)
                t_done = t6
                last_dv = t6
            else:
                t_done = p.op("vector", lambda E, dst=dst, src_off=src_off:
                              E.tensor_copy(dst, pbq[:, src_off:src_off + 128]), deps + [qr_free], dv)
                last_dv = t_done
            pslot = self.pb[2][:, 0:128] if nm == "q" else self.pb[6][:, 0:128]
            t_tp = p.op("tensor", lambda E, pslot=pslot, dst=dst:
                        E.transpose(pslot, dst, self.ident[:]),
                        [t_done, tp_free.get(nm, NONE_TOK), pb2_free], mm)
            th = (QTh if nm == "q" else KTh)[:, tt * 128:(tt + 1) * 128]
            tl = (QTl if nm == "q" else KTl)[:, tt * 128:(tt + 1) * 128]
            t_evh = p.op("scalar", lambda E, th=th, pslot=pslot: E.copy(th, pslot), [t_tp], ac)
            t_evl = p.op("vector", lambda E, tl=tl, pslot=pslot, th=th: E.tensor_sub(tl, pslot, th), [t_evh], dv)
            tp_free[nm] = t_evl
            last_dv = t_evl
        t_vh0 = p.op("scalar", lambda E, tt=tt: E.copy(Vh[:, tt, 0:64], pbq[:, 256:320]), deps, ac)
        t_vh1 = p.op("scalar", lambda E, tt=tt: E.copy(Vh[:, tt, 66:130], pbq[:, 320:384]), [], ac)
        t_vl0 = p.op("vector", lambda E, tt=tt: E.tensor_sub(Vl[:, tt, 0:64], pbq[:, 256:320], Vh[:, tt, 0:64]),
                     [t_vh0], dv)
        t_vl1 = p.op("vector", lambda E, tt=tt: E.tensor_sub(Vl[:, tt, 66:130], pbq[:, 320:384], Vh[:, tt, 66:130]),
                     [t_vh1], dv)
        out["free_psum"] = [t_vl1, t_vl0]
        out["qr_free"] = tp_free["k"]
        out["cs_free"] = last_dv
        return out

    def ffn_stage(self, arena_gate, first_call):
        p = self.P
        mm, dv, ac = self.mm, self.dv, self.ac
        f, C, T, TT = self.ft, self.C, self.T, self.TT
        xnT_tok = self.tok_xnT

        BF16 = mybir.dt.bfloat16
        hTh = C[:, f["hT"]:f["hT"] + 4096].bitcast(BF16)
        hTl = C[:, f["hT"] + 4096:f["hT"] + 8192].bitcast(BF16)
        tsb = self.xn_s[:, 0:512]
        gl = self.gsc

        if first_call:
            gsrc = self.gwd[:]
            t_g1 = p.dma("gpsimd", self.gwh[:], gsrc, waits=[])
            t_g2 = p.op("vector", lambda E: E.tensor_scalar(self.gwl[:], self.gwh[:], -1.0, None, ALU.mult), [t_g1], dv)
            t_g3 = p.dma("gpsimd", self.gwl[:], gsrc, waits=[t_g2], accum_op=ALU.add)
            self.t_gw = [t_g1, t_g3]
            self.t_gb = p.dma("sync", self.gbias[:], self.gbiasd[:])

        gate_done = {}
        gs_read_gate = self.gs_read
        for tt in range(TT):
            w0 = list(xnT_tok) + self.t_gw + [gate_done.get(tt % 2, arena_gate), gs_read_gate]
            t_g = None
            pbG = self.pb[6 + (tt % 2)]
            for k in range(8):
                xh = self.Ah[:, k * T + tt * 128:k * T + tt * 128 + 128]
                xl = self.Al[:, k * T + tt * 128:k * T + tt * 128 + 128]
                p.op("tensor", lambda E, pbG=pbG, xh=xh, k=k:
                     E.matmul(pbG[:, 0:8], xh, self.gwh[:, k * 8:(k + 1) * 8],
                              start=(k == 0), stop=False), w0 if k == 0 else [], mm)
                p.op("tensor", lambda E, pbG=pbG, xh=xh, k=k:
                     E.matmul(pbG[:, 0:8], xh, self.gwl[:, k * 8:(k + 1) * 8],
                              start=False, stop=False), [], mm)
                t_g = p.op("tensor", lambda E, pbG=pbG, xl=xl, k=k:
                           E.matmul(pbG[:, 0:8], xl, self.gwh[:, k * 8:(k + 1) * 8],
                                    start=False, stop=(k == 7)), [], mm)
            sl = pbG[:, 0:8]
            t1 = p.op("vector", lambda E, sl=sl: E.reduce_max(gl[:, 0:1], sl, axis=AXX), [t_g], dv)
            t2 = p.op("vector", lambda E: E.tensor_scalar(gl[:, 1:2], gl[:, 0:1], -1.0, None, ALU.mult), [t1], dv)
            t3 = p.op("scalar", lambda E, sl=sl: E.activation(gl[:, 2:10], sl, AF.Exp, bias=gl[:, 1:2],
                                                              accum_out=gl[:, 10:11]), [t2], ac)
            t4 = p.op("vector", lambda E: E.reciprocal(gl[:, 11:12], gl[:, 10:11]), [t3], dv)
            t5 = p.op("vector", lambda E: E.tensor_mul(gl[:, 12:13], gl[:, 10:11], gl[:, 11:12]), [t4], dv)
            t6 = p.op("vector", lambda E: E.tensor_scalar(gl[:, 12:13], gl[:, 12:13], -1.0, 2.0, ALU.mult, ALU.add), [t5], dv)
            t7 = p.op("vector", lambda E: E.tensor_mul(gl[:, 11:12], gl[:, 11:12], gl[:, 12:13]), [t6], dv)
            t8 = p.op("vector", lambda E: E.tensor_scalar(gl[:, 13:21], gl[:, 2:10], gl[:, 11:12], None, ALU.mult), [t7], dv)
            t9 = p.op("vector", lambda E: E.tensor_add(gl[:, 21:29], gl[:, 13:21], self.gbias[:]), [t8, self.t_gb], dv)
            ta = p.op("vector", lambda E: E.reduce_max(gl[:, 29:30], gl[:, 21:29], axis=AXX), [t9], dv)
            tb = p.op("vector", lambda E: E.tensor_scalar(gl[:, 30:38], gl[:, 21:29], gl[:, 29:30], None, ALU.is_ge), [ta], dv)
            tc = p.op("vector", lambda E: E.scalar_tensor_tensor(gl[:, 21:29], gl[:, 30:38], -1e9, gl[:, 21:29],
                                                                 ALU.mult, ALU.add), [tb], dv)
            td = p.op("vector", lambda E: E.reduce_max(gl[:, 29:30], gl[:, 21:29], axis=AXX), [tc], dv)
            te_ = p.op("vector", lambda E: E.tensor_scalar(gl[:, 21:29], gl[:, 21:29], gl[:, 29:30], None, ALU.is_ge), [td], dv)
            tf = p.op("vector", lambda E: E.tensor_add(gl[:, 30:38], gl[:, 30:38], gl[:, 21:29]), [te_], dv)
            tg = p.op("vector", lambda E, tt=tt: E.tensor_mul(self.GS[:, tt * 8:(tt + 1) * 8], gl[:, 13:21], gl[:, 30:38]), [tf], dv)
            gate_done[tt % 2] = tg
        gs_tok = p.tok(dv)
        if self.debug and first_call:
            dgs = self.nc.dram_tensor("dbg_GS", [128, self.TT * NE], F32, kind="ExternalOutput").ap()
            tdg = p.dma("sync", dgs, self.GS[:], waits=[gs_tok])
            p.op("sync", lambda E: None, waits=[tdg])

        w13_free = {0: arena_gate, 1: arena_gate, 2: arena_gate}
        w2_free = {}
        hT_free = {}
        h_ps_free = {}
        tsb_free = NONE_TOK
        w2ev = {i: NONE_TOK for i in range(8)}
        experts = [("r", e) for e in range(NE)] + [("s", 0)]
        TGS = min(512, T)
        NTI = TGS // 128
        n_tg = T // TGS
        w2_rr = [0]
        for tg0 in range(0, n_tg, 2):
            jobs = [("r", e, list(range(tg0, min(tg0 + 2, n_tg)))) for e in range(NE)]
            jobs += [("s", 0, [tgx]) for tgx in range(tg0, min(tg0 + 2, n_tg))]
            for kind, e, tg_list in jobs:
                ic_n = 8 if kind == "r" else 16
                h_done = {}
                pend = None
                views = {}
                hi_tok = {}
                for ici in range(ic_n + 1):
                    if ici < ic_n:
                        parx = ici % 3
                        w13 = C[:, f["w13"] + parx * 2048: f["w13"] + parx * 2048 + 2048]
                        whi = w13[:, 0:1024].bitcast(BF16)
                        wlo = w13[:, 1024:2048].bitcast(BF16)
                        s13 = self.w13d[e, ici] if kind == "r" else self.sw13d[ici]
                        t_hi = p.dma("gpsimd", whi[:], s13, waits=[w13_free[parx]])
                        t_ng = p.op("vector", lambda E, wlo=wlo, whi=whi:
                                    E.tensor_scalar(wlo[:], whi[:], -1.0, None, ALU.mult),
                                    [t_hi, w13_free[parx]], dv)
                        views[ici] = (whi, wlo, s13, t_ng)
                        hi_tok[ici] = t_hi
                    if ici >= 1:
                        ic = ici - 1
                        par = ic % 3
                        whi, wlo, s13, t_ng = views[ic]
                        t_acc = p.dma("gpsimd", wlo[:], s13, waits=[t_ng], accum_op=ALU.add)
                        w1h = whi[:, 0:1024]
                        w3h = whi[:, 1024:2048]
                        w1l = wlo[:, 0:1024]
                        w3l = wlo[:, 1024:2048]
                        w_w = [hi_tok[ic], t_acc]
                        for tgl, tg in enumerate(tg_list):
                            tokbase = tg * TGS
                            sp = (ic * len(tg_list) + tgl) % 4
                            pb1 = self.pb[sp]
                            pb3 = self.pb[4 + sp]
                            w0 = list(xnT_tok) + w_w + [h_ps_free.get(sp, gs_tok)]
                            t_h1 = t_h3 = None
                            for k in range(8):
                                xh = self.Ah[:, k * T + tokbase: k * T + tokbase + TGS]
                                xl = self.Al[:, k * T + tokbase: k * T + tokbase + TGS]
                                p.op("tensor", lambda E, pb1=pb1, xh=xh, w1h=w1h, k=k:
                                     E.matmul(pb1[:, 0:TGS], w1h[:, k * 128:(k + 1) * 128], xh,
                                              start=(k == 0), stop=False), w0 if k == 0 else [], mm)
                                p.op("tensor", lambda E, pb1=pb1, xl=xl, w1h=w1h, k=k:
                                     E.matmul(pb1[:, 0:TGS], w1h[:, k * 128:(k + 1) * 128], xl,
                                              start=False, stop=False), [], mm)
                                t_h1 = p.op("tensor", lambda E, pb1=pb1, xh=xh, w1l=w1l, k=k:
                                            E.matmul(pb1[:, 0:TGS], w1l[:, k * 128:(k + 1) * 128], xh,
                                                     start=False, stop=(k == 7)), [], mm)
                            for k in range(8):
                                xh = self.Ah[:, k * T + tokbase: k * T + tokbase + TGS]
                                xl = self.Al[:, k * T + tokbase: k * T + tokbase + TGS]
                                p.op("tensor", lambda E, pb3=pb3, xh=xh, w3h=w3h, k=k:
                                     E.matmul(pb3[:, 0:TGS], w3h[:, k * 128:(k + 1) * 128], xh,
                                              start=(k == 0), stop=False), [], mm)
                                p.op("tensor", lambda E, pb3=pb3, xl=xl, w3h=w3h, k=k:
                                     E.matmul(pb3[:, 0:TGS], w3h[:, k * 128:(k + 1) * 128], xl,
                                              start=False, stop=False), [], mm)
                                t_h3 = p.op("tensor", lambda E, pb3=pb3, xh=xh, w3l=w3l, k=k:
                                            E.matmul(pb3[:, 0:TGS], w3l[:, k * 128:(k + 1) * 128], xh,
                                                     start=False, stop=(k == 7)), [], mm)
                            t_si = p.op("scalar", lambda E, pb1=pb1: E.activation(tsb[:, 0:TGS], pb1[:, 0:TGS], AF.Silu),
                                        [t_h1, tsb_free], ac)
                            t_mu = p.op("vector", lambda E, pb3=pb3:
                                        E.tensor_mul(tsb[:, 0:TGS], tsb[:, 0:TGS], pb3[:, 0:TGS]), [t_si, t_h3], dv)
                            hoff = ic * len(tg_list) * TGS + tgl * TGS
                            hsh = hTh[:, hoff: hoff + TGS]
                            hsl = hTl[:, hoff: hoff + TGS]
                            t_hh = p.op("scalar", lambda E, hsh=hsh: E.copy(hsh, tsb[:, 0:TGS]),
                                        [t_mu, hT_free.get((ic, tgl), arena_gate)], ac)
                            t_h = p.op("vector", lambda E, hsl=hsl, hsh=hsh: E.tensor_sub(hsl, tsb[:, 0:TGS], hsh),
                                       [t_hh], dv)
                            tsb_free = t_h
                            h_ps_free[sp] = t_h
                            h_done[(ic, tgl)] = t_h
                        w13_free[par] = t_h
                h_all = p.tok(dv)
                for half in range(2):
                    w2v = {}
                    w2hi_tok = {}
                    for ici in range(ic_n + 1):
                        if ici < ic_n:
                            slot = w2_rr[0] % 3
                            w2_rr[0] += 1
                            w2r = C[:, f["w2"] + slot * 512: f["w2"] + slot * 512 + 512]
                            w2h = w2r[:, 0:256].bitcast(BF16)
                            w2l = w2r[:, 256:512].bitcast(BF16)
                            if kind == "r":
                                src2 = self.w2d[e, ici * 128:(ici + 1) * 128, half * 512:(half + 1) * 512]
                            else:
                                src2 = self.sw2d[ici * 128:(ici + 1) * 128, half * 512:(half + 1) * 512]
                            t_a2 = p.dma("gpsimd", w2h, src2, waits=[w2_free.get(slot, arena_gate)])
                            t_b2 = p.op("vector", lambda E, w2l=w2l, w2h=w2h:
                                        E.tensor_scalar(w2l, w2h, -1.0, None, ALU.mult),
                                        [t_a2, w2_free.get(slot, arena_gate)], dv)
                            w2v[ici] = (w2h, w2l, src2, t_b2, slot)
                            w2hi_tok[ici] = t_a2
                        if ici >= 1:
                            ic = ici - 1
                            w2h, w2l, src2, t_b2, slot = w2v[ic]
                            t_c2 = p.dma("gpsimd", w2l, src2, waits=[t_b2], accum_op=ALU.add)
                            for tgl in range(len(tg_list)):
                                for ti in range(NTI):
                                    bank = tgl * NTI + ti
                                    pbW = self.pb[bank]
                                    hoff = ic * len(tg_list) * TGS + tgl * TGS + ti * 128
                                    lh = hTh[:, hoff: hoff + 128]
                                    ll = hTl[:, hoff: hoff + 128]
                                    wde = [h_done[(ic, tgl)], w2hi_tok[ic], t_c2]
                                    if ic == 0:
                                        wde.append(w2ev[bank])
                                    p.op("tensor", lambda E, pbW=pbW, lh=lh, w2h=w2h, s=(ic == 0):
                                         E.matmul(pbW[:, 0:512], lh, w2h, start=s, stop=False), wde, mm)
                                    p.op("tensor", lambda E, pbW=pbW, lh=lh, w2l=w2l:
                                         E.matmul(pbW[:, 0:512], lh, w2l, start=False, stop=False), [], mm)
                                    t_w = p.op("tensor", lambda E, pbW=pbW, ll=ll, w2h=w2h, st=(ic == ic_n - 1):
                                               E.matmul(pbW[:, 0:512], ll, w2h, start=False, stop=st), [], mm)
                            w2_free[slot] = t_w
                    for tgl, tg in enumerate(tg_list):
                        tokbase = tg * TGS
                        for ti in range(NTI):
                            bank = tgl * NTI + ti
                            tt_g = (tokbase // 128) + ti
                            pbW = self.pb[bank]
                            Bsl = self.B[:, tt_g * D + half * 512: tt_g * D + half * 512 + 512]
                            if kind == "r":
                                gcol = self.GS[:, tt_g * 8 + e: tt_g * 8 + e + 1]
                                t_ev = p.op("vector", lambda E, Bsl=Bsl, pbW=pbW, gcol=gcol:
                                            E.scalar_tensor_tensor(Bsl, pbW[:, 0:512], gcol, Bsl, ALU.mult, ALU.add),
                                            [t_w, gs_tok, self.tok_B], dv)
                            else:
                                t_ev = p.op("vector", lambda E, Bsl=Bsl, pbW=pbW:
                                            E.tensor_add(Bsl, Bsl, pbW[:, 0:512]), [t_w, self.tok_B], dv)
                            w2ev[bank] = t_ev
                mm_now = p.tok(mm)
                for ic in range(16):
                    for tg2 in range(2):
                        hT_free[(ic, tg2)] = mm_now
                last_ev = w2ev[max(k for k in w2ev if isinstance(k, int))]
                for sp in range(4):
                    h_ps_free[sp] = last_ev
        self.tok_B = p.tok(dv)
        self.gs_read = p.tok(dv)
        return p.tok(mm)

    def emit(self):
        p = self.P
        t_id = p.dma("sync", self.ident[:], self.identd[:])
        t_mq = p.dma("sync", self.maskqm[:], self.maskqmd[:])
        self.t_maskqm = t_mq
        t_on = p.op("vector", lambda E: E.memset(self.ones1[:], 1.0), [], self.dv)
        p.op("vector", lambda E: E.memset(self.ones_bf[:], 1.0), [], self.dv)
        self.t_const = p.op("vector", lambda E: E.memset(self.ck[:, 0:1], MSHIFT), [], self.dv)
        self.R_free_init = t_on
        self.gs_read = NONE_TOK
        t_x = p.dma("sync", self.B.rearrange("p (t d) -> p t d", d=D),
                    self.x_in.rearrange("(t p) d -> p t d", p=128))
        self.tok_B = t_x
        self.ln_dummy_free = NONE_TOK
        for eng in ("tensor", "vector", "scalar"):
            p.op(eng, lambda E: None, waits=[t_id, t_x, t_mq])

        self.ln_stage(extra=[t_x])
        self.dump_A("xnT1")
        g = self.fence()
        g = self.attn_stage(self.wqkv_a, self.wo_a, use_rope=True, causal=True, arena_gate=g)
        self.dump_B("x1")
        self.fence()
        self.ln_stage()
        g = self.fence()
        g = self.ffn_stage(arena_gate=g, first_call=True)
        self.dump_B("x2")
        self.fence()
        self.ln_stage()
        g = self.fence()
        g = self.attn_stage(self.wqkv_b, self.wo_b, use_rope=False, causal=False, arena_gate=g)
        self.dump_B("x3")
        self.fence()
        self.ln_stage()
        g = self.fence()
        g = self.ffn_stage(arena_gate=g, first_call=False)
        self.fence()

        t_y = p.dma("sync", self.y_out.rearrange("(t p) d -> p t d", p=128),
                    self.B.rearrange("p (t d) -> p t d", d=D), waits=[self.tok_B])
        p.op("sync", lambda E: None, waits=[t_y])
        p.build()


_BUILD_CACHE = {}


def build_nc(T=2048, detect_races=False, debug=False):
    key = (T, detect_races, debug)
    if key in _BUILD_CACHE:
        return _BUILD_CACHE[key]
    es = ExitStack()
    nc = bass.Bass("TRN2", target_bir_lowering=False, detect_race_conditions=detect_races)
    b = Builder(nc, es, T)
    b.debug = debug
    b.emit()
    es.close()
    _BUILD_CACHE[key] = nc
    return nc



def _wblk(wT):
    f32 = np.float32
    out = np.empty((NE, 8, 128, 1024), dtype=f32)
    for e in range(NE):
        for ic in range(8):
            out[e, ic] = wT[e][:, ic * 128:(ic + 1) * 128].reshape(8, 128, 128).transpose(1, 0, 2).reshape(128, 1024)
    return np.ascontiguousarray(out)


def _swblk(swT):
    f32 = np.float32
    out = np.empty((16, 128, 1024), dtype=f32)
    for ic in range(16):
        out[ic] = swT[:, ic * 128:(ic + 1) * 128].reshape(8, 128, 128).transpose(1, 0, 2).reshape(128, 1024)
    return np.ascontiguousarray(out)


def _stage_weights(T, inputs):
    f32 = np.float32
    g = lambda k: np.asarray(inputs[k], dtype=f32)

    def qkv_blocked(wqT, wkT, wvT):
        out = np.empty((8, 128, 3072), dtype=f32)
        for hp in range(8):
            blk = np.concatenate([wqT[:, hp * 128:(hp + 1) * 128],
                                  wkT[:, hp * 128:(hp + 1) * 128],
                                  wvT[:, hp * 128:(hp + 1) * 128]], axis=1)
            out[hp] = blk.reshape(8, 128, 384).transpose(1, 0, 2).reshape(128, 3072)
        return np.ascontiguousarray(out)

    wqkv_a = qkv_blocked(g("L0_attn_a_Wq").T, g("L0_attn_a_Wk").T, g("L0_attn_a_Wv").T)
    wo_a = np.ascontiguousarray(g("L0_attn_a_Wo").T)
    wqkvT = g("L0_attn_b_w_qkv").T
    wqkv_b = qkv_blocked(wqkvT[:, 0:1024], wqkvT[:, 1024:2048], wqkvT[:, 2048:3072])
    wo_b = np.ascontiguousarray(g("L0_attn_b_w_proj").T)

    pos = np.arange(T, dtype=np.float64)
    inv = 1.0 / (10000.0 ** (np.arange(0, DK, 2, dtype=np.float64) / DK))
    ang = pos[:, None] * inv[None, :]
    cos32 = np.cos(ang).astype(f32)
    sin32 = np.sin(ang).astype(f32)
    cosx = np.empty((T, 128), dtype=f32)
    sinx = np.empty((T, 128), dtype=f32)
    for hl in range(2):
        cosx[:, hl * 64 + 0::2][:, :32] = cos32
        cosx[:, hl * 64 + 1::2][:, :32] = cos32
        sinx[:, hl * 64 + 0::2][:, :32] = sin32
        sinx[:, hl * 64 + 1::2][:, :32] = sin32

    QGS = min(512, T)
    DQG = QGS // 128
    maskd = np.zeros((128, DQG * QGS), dtype=f32)
    for j in range(DQG):
        kloc = j * 128 + np.arange(128)[:, None]
        qloc = np.arange(QGS)[None, :]
        maskd[:, j * QGS:(j + 1) * QGS] = np.where(kloc > qloc, NEGBIG, 0.0)

    return {
        "wqkv_a": wqkv_a, "wo_a": wo_a, "wqkv_b": wqkv_b, "wo_b": wo_b,
        "cosx": cosx, "sinx": sinx, "maskd": maskd,
        "maskqmd": np.where(np.arange(128)[None, :] > np.arange(128)[:, None], NEGBIG, 0.0).astype(f32),
        "identd": np.eye(128, dtype=f32),
        "gwd": np.ascontiguousarray(g("L0_ffn_gate_w").T.reshape(8, 128, NE).transpose(1, 0, 2).reshape(128, 64)),
        "gbiasd": np.ascontiguousarray(np.broadcast_to(g("L0_ffn_gate_bias"), (128, NE)).copy()),
        "w13d": np.ascontiguousarray(np.concatenate(
            [_wblk(np.transpose(g("L0_ffn_w1"), (0, 2, 1))),
             _wblk(np.transpose(g("L0_ffn_w3"), (0, 2, 1)))], axis=3)),
        "w2d": np.ascontiguousarray(np.transpose(g("L0_ffn_w2"), (0, 2, 1))),
        "sw13d": np.ascontiguousarray(np.concatenate(
            [_swblk(g("L0_ffn_sw1").T), _swblk(g("L0_ffn_sw3").T)], axis=2)),
        "sw2d": np.ascontiguousarray(g("L0_ffn_sw2").T),
    }


def kernel(**inputs):
    x = np.asarray(inputs["L0_attn_a_x"], dtype=np.float32)
    B_, T, D_ = x.shape
    staged = _stage_weights(T, inputs)
    nc = build_nc(T=T, detect_races=False)
    in_maps = []
    for b in range(B_):
        m = dict(staged)
        m["x_in"] = np.ascontiguousarray(x[b])
        in_maps.append(m)
    res = run_bass_kernel_spmd(nc, in_maps, core_ids=list(range(B_)))
    out = np.stack([np.asarray(r["y_out"]) for r in res.results], axis=0)
    return out.astype(np.float32)


# version 18
# speedup vs baseline: 1.0379x; 1.0379x over previous
import numpy as np
from contextlib import ExitStack

import concourse.bass as bass
import concourse.mybir as mybir
from concourse.bass_utils import run_bass_kernel_spmd

F32 = mybir.dt.float32
AF = mybir.ActivationFunctionType
ALU = mybir.AluOpType
AXX = mybir.AxisListType.X

ENGS = ("tensor", "vector", "scalar", "sync", "gpsimd")

D = 1024
H = 16
DK = 64
NE = 8
SI = 2048
MSHIFT = -8.0
SCALE = 0.125
NEGBIG = -8.0e9
NONE_TOK = (None, 0)


class _Sem:
    __slots__ = ("h", "n", "name")


class Prog:

    def __init__(self, nc, es):
        self.nc = nc
        self.es = es
        self.ops = {e: [] for e in ENGS}
        self.qsem = {}

    def sem(self, name):
        s = _Sem()
        s.h = self.es.enter_context(self.nc.semaphore(name))
        s.n = 0
        s.name = name
        return s

    def sb(self, name, shape, dtype=F32):
        return self.es.enter_context(self.nc.sbuf_tensor(name, shape, dtype))

    def ps(self, name, shape, dtype=F32):
        return self.es.enter_context(self.nc.psum_tensor(name, shape, dtype))

    def op(self, eng, fn, waits=(), inc=None, n=1):
        waits = tuple((s, v) for (s, v) in waits if s is not None and v > 0)
        self.ops[eng].append((waits, fn, inc, n))
        if inc is not None:
            inc.n += n
            return (inc, inc.n)
        return None

    def dma(self, eng, out, in_, waits=(), **kw):
        sems = self.qsem[eng]
        i = self.qrr[eng]
        self.qrr[eng] = (i + 1) % len(sems)
        sem = sems[i]
        waits = list(waits) + [(sem, sem.n)]
        return self.op(eng, lambda E: E.dma_start(out=out, in_=in_, **kw), waits, sem, 16)

    def tok(self, sem):
        return (sem, sem.n)

    def build(self):
        with self.nc.Block() as blk:
            for eng in ENGS:
                getattr(blk, eng)(self._replayer(eng))

    def _replayer(self, eng):
        entries = self.ops[eng]

        def run(E):
            water = {}
            for waits, fn, inc, n in entries:
                pend = []
                for (s, v) in waits:
                    if water.get(id(s), 0) < v:
                        pend.append((s, v))
                        water[id(s)] = v
                attach = False
                for (s, v) in (pend[:-1] if attach else pend):
                    E.wait_ge(s.h, v)
                inst = fn(E)
                if inst is None:
                    if attach:
                        E.wait_ge(pend[-1][0].h, pend[-1][1])
                else:
                    if attach:
                        inst._wait_ge(pend[-1][0].h, pend[-1][1])
                    if inc is not None:
                        inst.then_inc(inc.h, n)
            return None

        return run


class Builder:
    def __init__(self, nc, es, T):
        self.nc = nc
        self.es = es
        self.T = T
        self.TT = T // 128
        self.KC = T // 128
        self.QGS = min(512, T)
        self.QG = T // self.QGS
        self.DQG = self.QGS // 128
        p = self.P = Prog(nc, es)

        self.mm = p.sem("mm")
        self.dv = p.sem("dv")
        self.ac = p.sem("ac")
        p.qsem["sync"] = [p.sem(f"qs{i}") for i in range(4)]
        p.qsem["gpsimd"] = [p.sem(f"qg{i}") for i in range(4)]
        p.qrr = {"sync": 0, "gpsimd": 0}

        dt = nc.dram_tensor
        self.x_in = dt("x_in", [T, D], F32, kind="ExternalInput").ap()
        self.y_out = dt("y_out", [T, D], F32, kind="ExternalOutput").ap()
        self.wqkv_a = dt("wqkv_a", [8, 128, 3072], F32, kind="ExternalInput").ap()
        self.wo_a = dt("wo_a", [D, D], F32, kind="ExternalInput").ap()
        self.wqkv_b = dt("wqkv_b", [8, 128, 3072], F32, kind="ExternalInput").ap()
        self.wo_b = dt("wo_b", [D, D], F32, kind="ExternalInput").ap()
        self.cosx = dt("cosx", [T, 128], F32, kind="ExternalInput").ap()
        self.sinx = dt("sinx", [T, 128], F32, kind="ExternalInput").ap()
        self.maskd = dt("maskd", [128, self.DQG * self.QGS], F32, kind="ExternalInput").ap()
        self.maskqmd = dt("maskqmd", [128, 128], F32, kind="ExternalInput").ap()
        self.identd = dt("identd", [128, 128], F32, kind="ExternalInput").ap()
        self.gwd = dt("gwd", [128, 64], F32, kind="ExternalInput").ap()
        self.gbiasd = dt("gbiasd", [128, NE], F32, kind="ExternalInput").ap()
        self.w13d = dt("w13d", [NE, 8, 128, 2048], F32, kind="ExternalInput").ap()
        self.w2d = dt("w2d", [NE, D, D], F32, kind="ExternalInput").ap()
        self.sw13d = dt("sw13d", [16, 128, 2048], F32, kind="ExternalInput").ap()
        self.sw2d = dt("sw2d", [SI, D], F32, kind="ExternalInput").ap()
        self.debug = False
        self.dbg = {}

        sb = p.sb
        self.Ah = sb("A_xnTh", [128, 8 * T], mybir.dt.bfloat16)
        self.Al = sb("A_xnTl", [128, 8 * T], mybir.dt.bfloat16)
        self.B = sb("B_x", [128, self.TT * D])
        self.C = sb("C_arena", [128, 15936])
        self.ident = sb("ident", [128, 128])
        self.gbias = sb("gbias", [128, NE])
        self.GS = sb("GS", [128, self.TT * NE])
        self.stats = sb("stats", [128, 32])
        self.xn_s = sb("xn_s", [128, D])
        self.rope_s = sb("rope_s", [128, 384])
        self.ones1 = sb("ones1", [1, 128])
        self.r_sb = sb("r_sb", [1, 1536])
        self.gsc = sb("gsc", [128, 40])
        self.gwh = sb("gwh", [128, 64], mybir.dt.bfloat16)
        self.gwl = sb("gwl", [128, 64], mybir.dt.bfloat16)
        self.ck = sb("ck", [128, 2])
        self.otr = self.xn_s[0:64, 0:512]
        self.maskqm = sb("maskqm", [128, 128])
        self.ones_bf = sb("ones_bf", [1, 128], mybir.dt.bfloat16)
        self.msb_bf = sb("msb_bf", [1, 1024], mybir.dt.bfloat16)

        self.pb = [p.ps(f"pb{i}", [128, 512]) for i in range(8)]

        a = {}
        o = 0
        for name, n in (("QTh", T // 2), ("QTl", T // 2), ("KTh", T // 2), ("KTl", T // 2),
                        ("Vh", self.KC * 66), ("Vl", self.KC * 66),
                        ("ET", 2 * self.QGS), ("OT0", self.QGS), ("OT1", self.QGS),
                        ("wqkv", 8 * 384), ("wo0", D), ("wo1", D),
                        ("cs", 256), ("mask", self.DQG * self.QGS), ("qr", 256)):
            a[name] = o
            o += n
        assert o <= 15936, o
        self.at = a
        f = {}
        o = 0
        for name, n in (("hT", 16 * 512), ("w13", 6 * 1024), ("w2", 3 * 512), ("tsb", 0)):
            f[name] = o
            o += n
        assert o <= 15936, o
        self.ft = f

    def dump(self, name, src_ap, shape):
        if not self.debug:
            return
        p = self.P
        d = self.nc.dram_tensor("dbg_" + name, shape, F32, kind="ExternalOutput").ap()
        toks = [p.tok(self.mm), p.tok(self.dv), p.tok(self.ac)]
        t = p.dma("sync", d, src_ap, waits=toks)
        p.op("sync", lambda E: None, waits=[t])

    def dump_B(self, name):
        if not self.debug:
            return
        p = self.P
        d = self.nc.dram_tensor("dbg_" + name, [self.T, D], F32, kind="ExternalOutput").ap()
        toks = [p.tok(self.mm), p.tok(self.dv), p.tok(self.ac)]
        t = p.dma("sync", d.rearrange("(t p) d -> p t d", p=128),
                  self.B.rearrange("p (t d) -> p t d", d=D), waits=toks)
        p.op("sync", lambda E: None, waits=[t])
        self.tok_B = p.op("vector", lambda E: E.engine_nop(), waits=[t], inc=self.dv)

    def dump_A(self, name):
        if not self.debug:
            return
        p = self.P
        d = self.nc.dram_tensor("dbg_" + name, [8, 128, self.T], F32, kind="ExternalOutput").ap()
        toks = [p.tok(self.mm), p.tok(self.dv), p.tok(self.ac)]
        t = p.dma("gpsimd", d.rearrange("c p t -> p c t"),
                  self.Ah.rearrange("p (c t) -> p c t", t=self.T), waits=toks)
        p.op("gpsimd", lambda E: None, waits=[t])

    def fence(self):
        p = self.P
        toks = [p.tok(self.mm), p.tok(self.dv), p.tok(self.ac)]
        for eng in ("tensor", "vector", "scalar"):
            p.op(eng, lambda E: None, waits=toks)
        return p.tok(self.mm)

    def ln_stage(self, extra=()):
        p = self.P
        mm, dv, ac = self.mm, self.dv, self.ac
        st = self.stats
        prev_tp = {}
        last_tp_mm = NONE_TOK
        for tt in range(self.TT):
            xt = self.B[:, tt * D:(tt + 1) * D]
            w = list(extra)
            t_s1 = p.op("vector", lambda E, xt=xt: E.reduce_sum(st[:, 0:1], xt, axis=AXX), w, dv)
            t_s2 = p.op("scalar", lambda E, xt=xt: E.activation(self.xn_s[:], xt, AF.Square, accum_out=st[:, 1:2]),
                        w + [last_tp_mm], ac)
            t_m = p.op("vector", lambda E: E.tensor_scalar(st[:, 2:3], st[:, 0:1], 1.0 / D, None, ALU.mult), [t_s1], dv)
            t_mm2 = p.op("vector", lambda E: E.tensor_mul(st[:, 3:4], st[:, 2:3], st[:, 2:3]), [t_m], dv)
            t_v = p.op("vector", lambda E: E.scalar_tensor_tensor(st[:, 4:5], st[:, 1:2], 1.0 / D, st[:, 3:4],
                                                                 ALU.mult, ALU.subtract), [t_s2, t_mm2], dv)
            t_ve = p.op("vector", lambda E: E.tensor_scalar(st[:, 5:6], st[:, 4:5], 1e-5, None, ALU.add), [t_v], dv)
            t_sq = p.op("scalar", lambda E: E.activation(st[:, 6:7], st[:, 5:6], AF.Sqrt), [t_ve], ac)
            tk = p.op("vector", lambda E: E.reciprocal(st[:, 7:8], st[:, 6:7]), [t_sq], dv)
            for _ in range(2):
                t_a = p.op("vector", lambda E: E.tensor_mul(st[:, 8:9], st[:, 7:8], st[:, 7:8]), [tk], dv)
                t_b = p.op("vector", lambda E: E.tensor_mul(st[:, 9:10], st[:, 5:6], st[:, 8:9]), [t_a], dv)
                t_c = p.op("vector", lambda E: E.tensor_scalar(st[:, 9:10], st[:, 9:10], -0.5, 1.5, ALU.mult, ALU.add), [t_b], dv)
                tk = p.op("vector", lambda E: E.tensor_mul(st[:, 7:8], st[:, 7:8], st[:, 9:10]), [t_c], dv)
            t_xn = p.op("vector", lambda E, xt=xt: E.tensor_scalar(self.xn_s[:], xt, st[:, 2:3], st[:, 7:8],
                                                                  ALU.subtract, ALU.mult), [tk, t_s2], dv)
            for fc in range(8):
                sl = fc % 2
                pslot = self.pb[6 + sl][:, 0:128]
                dsth = self.Ah[:, fc * self.T + tt * 128: fc * self.T + tt * 128 + 128]
                dstl = self.Al[:, fc * self.T + tt * 128: fc * self.T + tt * 128 + 128]
                wt = [t_xn, prev_tp.get(sl, NONE_TOK)]
                t_tp = p.op("tensor", lambda E, pslot=pslot, fc=fc:
                            E.transpose(pslot, self.xn_s[:, fc * 128:(fc + 1) * 128], self.ident[:]), wt, mm)
                t_evh = p.op("scalar", lambda E, pslot=pslot, dsth=dsth: E.copy(dsth, pslot), [t_tp], ac)
                t_evl = p.op("vector", lambda E, pslot=pslot, dsth=dsth, dstl=dstl:
                             E.tensor_sub(dstl, pslot, dsth), [t_evh], dv)
                prev_tp[sl] = t_evl
                last_tp_mm = t_tp
        self.tok_xnT = [p.tok(self.ac), p.tok(self.dv)]

    def attn_stage(self, wqkv_d, wo_d, use_rope, causal, arena_gate):
        p = self.P
        mm, dv, ac = self.mm, self.dv, self.ac
        a, C, T = self.at, self.C, self.T
        TT, KC, QGS, QG, DQG = self.TT, self.KC, self.QGS, self.QG, self.DQG
        BF16 = mybir.dt.bfloat16
        xnT_tok = self.tok_xnT

        QTh = C[:, a["QTh"]:a["QTh"] + T // 2].bitcast(BF16)
        QTl = C[:, a["QTl"]:a["QTl"] + T // 2].bitcast(BF16)
        KTh = C[:, a["KTh"]:a["KTh"] + T // 2].bitcast(BF16)
        KTl = C[:, a["KTl"]:a["KTl"] + T // 2].bitcast(BF16)
        Vh = C[:, a["Vh"]:a["Vh"] + KC * 66].bitcast(BF16).rearrange("p (k c) -> p k c", c=132)
        Vl = C[:, a["Vl"]:a["Vl"] + KC * 66].bitcast(BF16).rearrange("p (k c) -> p k c", c=132)
        wqh = C[:, a["wqkv"]:a["wqkv"] + 1536].bitcast(mybir.dt.bfloat16)
        wql = C[:, a["wqkv"] + 1536:a["wqkv"] + 3072].bitcast(mybir.dt.bfloat16)
        woh = [C[0:64, a["wo0"]:a["wo0"] + 512].bitcast(BF16),
               C[0:64, a["wo1"]:a["wo1"] + 512].bitcast(BF16)]
        wol = [C[0:64, a["wo0"] + 512:a["wo0"] + 1024].bitcast(BF16),
               C[0:64, a["wo1"] + 512:a["wo1"] + 1024].bitcast(BF16)]
        masks = C[:, a["mask"]:a["mask"] + DQG * QGS]
        qr = [C[:, a["qr"]:a["qr"] + 128], C[:, a["qr"] + 128:a["qr"] + 256]]

        if causal:
            t_mask = p.dma("sync", masks, self.maskd[:], waits=[arena_gate])
        else:
            t_mask = NONE_TOK
        t_ones0 = p.op("vector", lambda E: E.memset(Vh[:, :, 64:65], 1.0), [arena_gate] + xnT_tok, dv)
        t_ones1 = p.op("vector", lambda E: E.memset(Vh[:, :, 130:131], 1.0), [], dv)
        t_ones2 = p.op("vector", lambda E: E.memset(Vl[:, :, 64:65], 0.0), [], dv)
        t_ones3 = p.op("vector", lambda E: E.memset(Vl[:, :, 130:131], 0.0), [], dv)
        t_onesV = [t_ones0, t_ones1, t_ones2, t_ones3]

        prev_hp_done = NONE_TOK
        prev_hp_dv = NONE_TOK
        prev_wo_ev = NONE_TOK
        pb2_free = NONE_TOK
        et_free = [NONE_TOK] * 6
        exp_done = {}
        tp_free = {}
        epi = {}
        qr_free = NONE_TOK
        ot_evac = {}
        R_free = self.R_free_init
        woev = {}
        self.cs_free_tok = arena_gate
        self.otr_free = NONE_TOK
        self.msb_free = {}
        rn_tok = {}
        sc2_free = NONE_TOK

        for hp in range(8):
            wq_src = wqkv_d[hp]
            t_wqh = p.dma("gpsimd", wqh[:], wq_src,
                          waits=[arena_gate if hp == 0 else prev_hp_done])
            t_wqn = p.op("vector", lambda E: E.tensor_scalar(wql[:], wqh[:], -1.0, None, ALU.mult),
                         [t_wqh, prev_hp_done if hp else arena_gate], dv)
            rows0 = wo_d[hp * 128: hp * 128 + 64, :]
            rows1 = wo_d[hp * 128 + 64: hp * 128 + 128, :]
            t_h0 = p.dma("gpsimd", woh[0], rows0, waits=[prev_wo_ev])
            t_n0 = p.op("vector", lambda E: E.tensor_scalar(wol[0], woh[0], -1.0, None, ALU.mult),
                        [t_h0, prev_wo_ev], dv)
            t_h1 = p.dma("gpsimd", woh[1], rows1, waits=[prev_wo_ev])
            t_n1 = p.op("vector", lambda E: E.tensor_scalar(wol[1], woh[1], -1.0, None, ALU.mult),
                        [t_h1, prev_wo_ev], dv)
            t_wqla = p.dma("gpsimd", wql[:, 0:1536], wq_src[:, 0:1536],
                           waits=[t_wqn], accum_op=ALU.add)
            t_wqlb = p.dma("gpsimd", wql[:, 1536:3072], wq_src[:, 1536:3072],
                           waits=[t_wqn], accum_op=ALU.add)
            t_l0 = p.dma("gpsimd", wol[0], rows0, waits=[t_n0], accum_op=ALU.add)
            t_l1 = p.dma("gpsimd", wol[1], rows1, waits=[t_n1], accum_op=ALU.add)
            t_wq = [t_wqh, t_wqla, t_wqlb]
            t_wo = [t_h0, t_l0, t_h1, t_l1]

            for tt in range(TT):
                pbq = self.pb[tt % 2]
                waits = list(xnT_tok) + t_wq + epi.get(tt - 2, [prev_hp_done, prev_hp_dv])
                tk = None
                for k in range(8):
                    xh = self.Ah[:, k * T + tt * 128:k * T + tt * 128 + 128]
                    xl = self.Al[:, k * T + tt * 128:k * T + tt * 128 + 128]
                    rh = wqh[:, k * 384:(k + 1) * 384]
                    rl = wql[:, k * 384:(k + 1) * 384]
                    p.op("tensor", lambda E, pbq=pbq, xh=xh, rh=rh, k=k:
                         E.matmul(pbq[:, 0:384], xh, rh, start=(k == 0), stop=False),
                         waits if k == 0 else [], mm)
                    p.op("tensor", lambda E, pbq=pbq, xh=xh, rl=rl:
                         E.matmul(pbq[:, 0:384], xh, rl, start=False, stop=False), [], mm)
                    tk = p.op("tensor", lambda E, pbq=pbq, xl=xl, rh=rh, k=k:
                              E.matmul(pbq[:, 0:384], xl, rh, start=False, stop=(k == 7)), [], mm)
                deps = [tk]
                cs_aps = (None, None)
                if use_rope:
                    csl = C[:, a["cs"]:a["cs"] + 128]
                    snl = C[:, a["cs"] + 128:a["cs"] + 256]
                    t_c = p.dma("sync", csl, self.cosx[tt * 128:(tt + 1) * 128, :], waits=[self.cs_free_tok])
                    t_s = p.dma("sync", snl, self.sinx[tt * 128:(tt + 1) * 128, :], waits=[self.cs_free_tok])
                    deps += [t_c, t_s]
                    cs_aps = (csl, snl)
                toks = self._qkv_epilogue(pbq, tt, use_rope, deps, cs_aps, qr, qr_free,
                                          tp_free, pb2_free, (QTh, QTl, KTh, KTl), (Vh, Vl))
                epi[tt] = toks["free_psum"]
                qr_free = toks["qr_free"]
                if use_rope:
                    self.cs_free_tok = toks["cs_free"]
            qt_ready = [p.tok(self.ac), p.tok(self.dv)]

            for qg in range(QG):
                ot_toks = {}
                kc_hi = (qg + 1) * DQG if causal else KC
                mtoks_hl = {}
                for hl in range(2):
                    mtoks = []
                    Msb = self.msb_bf[0:1, hl * 512:hl * 512 + QGS]
                    for ti in range(DQG):
                        qt = qg * DQG + ti
                        hi_k = (qt + 1) * 128 if causal else T
                        kgs = (hi_k + 511) // 512
                        mcol = self.stats[:, 12 + hl * 8 + ti:13 + hl * 8 + ti]
                        tcol = self.stats[:, 16 + hl * 8 + ti:17 + hl * 8 + ti]
                        t_red = NONE_TOK
                        for kg in range(kgs):
                            n_v = min(512, hi_k - kg * 512)
                            par = kg % 2
                            pbA = self.pb[3 + par]
                            qsl = QTh[hl * 64:(hl + 1) * 64, qt * 128:(qt + 1) * 128]
                            ksl = KTh[hl * 64:(hl + 1) * 64, kg * 512:kg * 512 + n_v]
                            t_a = p.op("tensor", lambda E, pbA=pbA, qsl=qsl, ksl=ksl, n_v=n_v:
                                       E.matmul(pbA[:, 0:n_v], qsl, ksl, start=True, stop=True),
                                       qt_ready + [exp_done.get(par, NONE_TOK)], mm)
                            if causal and kg == kgs - 1:
                                off = qt * 128 - kg * 512
                                t_a = p.op("vector", lambda E, pbA=pbA, off=off:
                                           E.tensor_add(pbA[:, off:off + 128], pbA[:, off:off + 128], self.maskqm[:]),
                                           [t_a, self.t_maskqm], dv)
                            if kg == 0:
                                t_red = p.op("vector", lambda E, pbA=pbA, n_v=n_v, mcol=mcol:
                                             E.reduce_max(mcol, pbA[:, 0:n_v], axis=AXX),
                                             [t_a, mtoks[-1] if mtoks else NONE_TOK], dv)
                            else:
                                t_r1 = p.op("vector", lambda E, pbA=pbA, n_v=n_v, tcol=tcol:
                                            E.reduce_max(tcol, pbA[:, 0:n_v], axis=AXX), [t_a], dv)
                                t_red = p.op("vector", lambda E, mcol=mcol, tcol=tcol:
                                             E.tensor_max(mcol, mcol, tcol), [t_r1, t_red], dv)
                            exp_done[par] = t_red
                        t_tp = p.op("tensor", lambda E, mcol=mcol: E.transpose(self.pb[6][0:1, 0:128], mcol, self.ident[:]),
                                    [t_red, R_free], mm)
                        t_ng = p.op("vector", lambda E, Msb=Msb, ti=ti: E.tensor_scalar(
                                        Msb[0:1, ti * 128:ti * 128 + 128], self.pb[6][0:1, 0:128], -1.0, None, ALU.mult),
                                    [t_tp, self.msb_free.get(hl, NONE_TOK)], dv)
                        mtoks.append(t_ng)
                    mtoks_hl[hl] = mtoks
                pv_tok = {}
                for hl in range(2):
                    pbO = self.pb[5] if hl == 0 else self.pb[7]
                    mtoks = mtoks_hl[hl]
                    Msb = self.msb_bf[0:1, hl * 512:hl * 512 + QGS]
                    ex_tok = {}
                    for kc in range(kc_hi + 1):
                        if kc < kc_hi:
                            ssl = kc % 2
                            pbS = self.pb[3 + ssl]
                            kth = KTh[hl * 64:(hl + 1) * 64, kc * 128:(kc + 1) * 128]
                            ktl = KTl[hl * 64:(hl + 1) * 64, kc * 128:(kc + 1) * 128]
                            qth = QTh[hl * 64:(hl + 1) * 64, qg * QGS:(qg + 1) * QGS]
                            qtl = QTl[hl * 64:(hl + 1) * 64, qg * QGS:(qg + 1) * QGS]
                            w_s = qt_ready + [exp_done.get(ssl, NONE_TOK)]
                            p.op("tensor", lambda E, pbS=pbS, kth=kth, qth=qth:
                                 E.matmul(pbS[:, 0:QGS], kth, qth, start=True, stop=False), w_s, mm)
                            p.op("tensor", lambda E, pbS=pbS, kth=kth, qtl=qtl:
                                 E.matmul(pbS[:, 0:QGS], kth, qtl, start=False, stop=False), [], mm)
                            p.op("tensor", lambda E, pbS=pbS, ktl=ktl, qth=qth:
                                 E.matmul(pbS[:, 0:QGS], ktl, qth, start=False, stop=False), [], mm)
                            t_sc = p.op("tensor", lambda E, pbS=pbS, Msb=Msb:
                                        E.matmul(pbS[:, 0:QGS], self.ones_bf[:], Msb, start=False, stop=True),
                                        mtoks, mm)
                            self.msb_free[hl] = t_sc
                            if causal and kc >= qg * DQG:
                                j = kc - qg * DQG
                                t_sc = p.op("vector", lambda E, pbS=pbS, j=j:
                                            E.tensor_add(pbS[:, 0:QGS], pbS[:, 0:QGS], masks[:, j * QGS:(j + 1) * QGS]),
                                            [t_sc, t_mask], dv)
                            n_et = 2 if causal else 6
                            sl = kc % n_et
                            pbE = self.pb[sl % 2]
                            eoff = (a["ET"] + sl * QGS) if sl < 2 else (a["mask"] + (sl - 2) * QGS)
                            eth = C[:, eoff: eoff + QGS // 2].bitcast(BF16)[:, 0:QGS]
                            etl = C[:, eoff + QGS // 2: eoff + QGS].bitcast(BF16)[:, 0:QGS]
                            t_e0 = p.op("scalar", lambda E, pbE=pbE, pbS=pbS:
                                        E.activation(pbE[:, 0:QGS], pbS[:, 0:QGS], AF.Exp, scale=SCALE),
                                        [t_sc, et_free[sl]], ac)
                            t_e1 = p.op("vector", lambda E, eth=eth, pbE=pbE:
                                        E.tensor_copy(eth, pbE[:, 0:QGS]), [t_e0], dv)
                            t_e2 = p.op("vector", lambda E, etl=etl, pbE=pbE, eth=eth:
                                        E.tensor_sub(etl, pbE[:, 0:QGS], eth), [t_e1], dv)
                            exp_done[ssl] = t_e2
                            ex_tok[kc] = (t_e1, t_e2, eth, etl)
                        if kc >= 1:
                            kp = kc - 1
                            t_eh, t_el, eth, etl = ex_tok.pop(kp)
                            vvh = Vh[:, kp, 66 * hl:66 * hl + 65]
                            vvl = Vl[:, kp, 66 * hl:66 * hl + 65]
                            w_pv1 = [t_eh] + t_onesV
                            if kp == 0:
                                w_pv1.append(ot_evac.get(hl, NONE_TOK))
                                if hl == 1:
                                    w_pv1.append(woev.get(0, NONE_TOK))
                            p.op("tensor", lambda E, pbO=pbO, vvh=vvh, eth=eth, s=(kp == 0):
                                 E.matmul(pbO[0:65, 0:QGS], vvh, eth, start=s, stop=False), w_pv1, mm)
                            p.op("tensor", lambda E, pbO=pbO, vvl=vvl, eth=eth:
                                 E.matmul(pbO[0:65, 0:QGS], vvl, eth, start=False, stop=False), [], mm)
                            t_pv = p.op("tensor", lambda E, pbO=pbO, vvh=vvh, etl=etl, e=(kp == kc_hi - 1):
                                        E.matmul(pbO[0:65, 0:QGS], vvh, etl, start=False, stop=e), [t_el], mm)
                            et_free[kp % (2 if causal else 6)] = t_pv
                    pv_tok[hl] = t_pv
                    rrow = self.r_sb[0:1, hl * 512:hl * 512 + QGS]
                    sc2 = self.r_sb[0:1, 1024:1024 + QGS]
                    t_r = p.op("vector", lambda E, rrow=rrow, pbO=pbO: E.reciprocal(rrow, pbO[64:65, 0:QGS]),
                               [t_pv, rn_tok.get(hl, NONE_TOK)], dv)
                    t_t = p.op("vector", lambda E, sc2=sc2, rrow=rrow, pbO=pbO:
                               E.tensor_mul(sc2, pbO[64:65, 0:QGS], rrow), [t_r, sc2_free], dv)
                    t_u = p.op("vector", lambda E, sc2=sc2: E.tensor_scalar(sc2, sc2, -1.0, 2.0, ALU.mult, ALU.add), [t_t], dv)
                    t_rn = p.op("vector", lambda E, rrow=rrow, sc2=sc2: E.tensor_mul(rrow, rrow, sc2), [t_u], dv)
                    sc2_free = t_rn
                    rn_tok[hl] = t_rn
                for hl in range(2):
                    pbO = self.pb[5] if hl == 0 else self.pb[7]
                    t_pv = pv_tok[hl]
                    rrow = self.r_sb[0:1, hl * 512:hl * 512 + QGS]
                    t_rn = rn_tok[hl]
                    pbR = self.pb[6]
                    t_R = p.op("tensor", lambda E, pbR=pbR, rrow=rrow:
                               E.matmul(pbR[0:64, 0:QGS], self.ones1[0:1, 0:64], rrow, start=True, stop=True),
                               [t_rn, R_free], mm)
                    kk = "OT0" if hl == 0 else "OT1"
                    OTh = C[0:64, a[kk]:a[kk] + QGS // 2].bitcast(BF16)[:, 0:QGS]
                    OTl = C[0:64, a[kk] + QGS // 2:a[kk] + QGS].bitcast(BF16)[:, 0:QGS]
                    otr = self.xn_s[0:64, 0:512]
                    otn = self.xn_s[0:64, 512:1024]
                    t_cp = p.op("scalar", lambda E, pbO=pbO, otr=otr: E.copy(otr[:, 0:QGS], pbO[0:64, 0:QGS]),
                                [t_pv, self.otr_free], ac)
                    t_nm = p.op("vector", lambda E, otn=otn, otr=otr, pbR=pbR:
                                E.tensor_mul(otn[:, 0:QGS], otr[:, 0:QGS], pbR[0:64, 0:QGS]),
                                [t_R, t_cp], dv)
                    t_oh = p.op("scalar", lambda E, OTh=OTh, otn=otn: E.copy(OTh, otn[:, 0:QGS]),
                                [t_nm, ot_toks.get(("wo", hl), woev.get("last", NONE_TOK))], ac)
                    t_ol = p.op("vector", lambda E, OTl=OTl, otn=otn, OTh=OTh: E.tensor_sub(OTl, otn[:, 0:QGS], OTh),
                                [t_oh], dv)
                    self.otr_free = t_ol
                    R_free = t_ol
                    ot_evac[hl] = t_ol
                    ot_toks[hl] = [t_oh, t_ol]
                for ti in range(DQG):
                    tt_g = qg * DQG + ti
                    for half in range(2):
                        par = (ti * 2 + half) % 2
                        pbW = self.pb[7] if par == 0 else self.pb[2]
                        w0 = ot_toks[0] + ot_toks[1] + t_wo + [woev.get(par, NONE_TOK)]
                        t_w2 = None
                        for hl in range(2):
                            kk = "OT0" if hl == 0 else "OT1"
                            oh = C[0:64, a[kk]:a[kk] + QGS // 2].bitcast(BF16)[:, ti * 128:ti * 128 + 128]
                            ol = C[0:64, a[kk] + QGS // 2:a[kk] + QGS].bitcast(BF16)[:, ti * 128:ti * 128 + 128]
                            wh = woh[hl][:, half * 512:(half + 1) * 512]
                            wl = wol[hl][:, half * 512:(half + 1) * 512]
                            p.op("tensor", lambda E, pbW=pbW, oh=oh, wh=wh, s=(hl == 0):
                                 E.matmul(pbW[:, 0:512], oh, wh, start=s, stop=False), w0 if hl == 0 else [], mm)
                            p.op("tensor", lambda E, pbW=pbW, oh=oh, wl=wl:
                                 E.matmul(pbW[:, 0:512], oh, wl, start=False, stop=False), [], mm)
                            t_w2 = p.op("tensor", lambda E, pbW=pbW, ol=ol, wh=wh, e=(hl == 1):
                                        E.matmul(pbW[:, 0:512], ol, wh, start=False, stop=e), [], mm)
                        Bsl = self.B[:, tt_g * D + half * 512: tt_g * D + half * 512 + 512]
                        t_ev = p.op("vector", lambda E, Bsl=Bsl, pbW=pbW:
                                    E.tensor_add(Bsl, Bsl, pbW[:, 0:512]), [t_w2, self.tok_B], dv)
                        woev[par] = t_ev
                        woev["last"] = t_ev
                        prev_wo_ev = t_ev
                        if par == 1:
                            pb2_free = t_ev
                ot_toks[("wo", 0)] = woev["last"]
                ot_toks[("wo", 1)] = woev["last"]
            prev_hp_done = p.tok(mm)
            prev_hp_dv = p.tok(dv)
        self.tok_B = p.tok(dv)
        return p.tok(mm)

    def _qkv_epilogue(self, pbq, tt, use_rope, deps, cs_aps, qr, qr_free, tp_free, pb2_free, qk_t, v_t):
        p = self.P
        mm, dv, ac = self.mm, self.dv, self.ac
        QTh, QTl, KTh, KTl = qk_t
        Vh, Vl = v_t
        cs_t, sn_t = cs_aps
        rs = self.rope_s
        out = {}
        last_dv = NONE_TOK
        for idx, (src_off, nm) in enumerate(((0, "q"), (128, "k"))):
            dst = qr[idx]
            ro = idx * 192
            if use_rope:
                qe = pbq[:, src_off:src_off + 128][:, 0::2]
                qo = pbq[:, src_off:src_off + 128][:, 1::2]
                ce = cs_t[:, 0::2]
                co = cs_t[:, 1::2]
                se = sn_t[:, 0::2]
                so = sn_t[:, 1::2]
                de = dst[:, 0::2]
                do = dst[:, 1::2]
                w0 = deps + [qr_free]
                t1 = p.op("vector", lambda E, de=de, qe=qe, ce=ce: E.tensor_mul(de, qe, ce), w0, dv)
                t2 = p.op("vector", lambda E, qo=qo, se=se, ro=ro: E.tensor_mul(rs[:, ro:ro + 64], qo, se), [], dv)
                t3 = p.op("vector", lambda E, de=de, ro=ro: E.tensor_sub(de, de, rs[:, ro:ro + 64]), [t1, t2], dv)
                t4 = p.op("vector", lambda E, qe=qe, so=so, ro=ro: E.tensor_mul(rs[:, ro + 64:ro + 128], qe, so), [], dv)
                t5 = p.op("vector", lambda E, qo=qo, co=co, ro=ro: E.tensor_mul(rs[:, ro + 128:ro + 192], qo, co), [], dv)
                t6 = p.op("vector", lambda E, do=do, ro=ro:
                          E.tensor_add(do, rs[:, ro + 64:ro + 128], rs[:, ro + 128:ro + 192]), [t4, t5], dv)
                t_done = t6
                last_dv = t6
            else:
                t_done = p.op("vector", lambda E, dst=dst, src_off=src_off:
                              E.tensor_copy(dst, pbq[:, src_off:src_off + 128]), deps + [qr_free], dv)
                last_dv = t_done
            pslot = self.pb[2][:, 0:128] if nm == "q" else self.pb[6][:, 0:128]
            t_tp = p.op("tensor", lambda E, pslot=pslot, dst=dst:
                        E.transpose(pslot, dst, self.ident[:]),
                        [t_done, tp_free.get(nm, NONE_TOK), pb2_free], mm)
            th = (QTh if nm == "q" else KTh)[:, tt * 128:(tt + 1) * 128]
            tl = (QTl if nm == "q" else KTl)[:, tt * 128:(tt + 1) * 128]
            t_evh = p.op("scalar", lambda E, th=th, pslot=pslot: E.copy(th, pslot), [t_tp], ac)
            t_evl = p.op("vector", lambda E, tl=tl, pslot=pslot, th=th: E.tensor_sub(tl, pslot, th), [t_evh], dv)
            tp_free[nm] = t_evl
            last_dv = t_evl
        t_vh0 = p.op("scalar", lambda E, tt=tt: E.copy(Vh[:, tt, 0:64], pbq[:, 256:320]), deps, ac)
        t_vh1 = p.op("scalar", lambda E, tt=tt: E.copy(Vh[:, tt, 66:130], pbq[:, 320:384]), [], ac)
        t_vl0 = p.op("vector", lambda E, tt=tt: E.tensor_sub(Vl[:, tt, 0:64], pbq[:, 256:320], Vh[:, tt, 0:64]),
                     [t_vh0], dv)
        t_vl1 = p.op("vector", lambda E, tt=tt: E.tensor_sub(Vl[:, tt, 66:130], pbq[:, 320:384], Vh[:, tt, 66:130]),
                     [t_vh1], dv)
        out["free_psum"] = [t_vl1, t_vl0]
        out["qr_free"] = tp_free["k"]
        out["cs_free"] = last_dv
        return out

    def ffn_stage(self, arena_gate, first_call):
        p = self.P
        mm, dv, ac = self.mm, self.dv, self.ac
        f, C, T, TT = self.ft, self.C, self.T, self.TT
        xnT_tok = self.tok_xnT

        BF16 = mybir.dt.bfloat16
        hTh = C[:, f["hT"]:f["hT"] + 4096].bitcast(BF16)
        hTl = C[:, f["hT"] + 4096:f["hT"] + 8192].bitcast(BF16)
        tsb = self.xn_s[:, 0:512]
        gl = self.gsc

        if first_call:
            gsrc = self.gwd[:]
            t_g1 = p.dma("gpsimd", self.gwh[:], gsrc, waits=[])
            t_g2 = p.op("vector", lambda E: E.tensor_scalar(self.gwl[:], self.gwh[:], -1.0, None, ALU.mult), [t_g1], dv)
            t_g3 = p.dma("gpsimd", self.gwl[:], gsrc, waits=[t_g2], accum_op=ALU.add)
            self.t_gw = [t_g1, t_g3]
            self.t_gb = p.dma("sync", self.gbias[:], self.gbiasd[:])

        gate_done = {}
        gs_read_gate = self.gs_read
        for tt in range(TT):
            w0 = list(xnT_tok) + self.t_gw + [gate_done.get(tt % 2, arena_gate), gs_read_gate]
            t_g = None
            pbG = self.pb[6 + (tt % 2)]
            for k in range(8):
                xh = self.Ah[:, k * T + tt * 128:k * T + tt * 128 + 128]
                xl = self.Al[:, k * T + tt * 128:k * T + tt * 128 + 128]
                p.op("tensor", lambda E, pbG=pbG, xh=xh, k=k:
                     E.matmul(pbG[:, 0:8], xh, self.gwh[:, k * 8:(k + 1) * 8],
                              start=(k == 0), stop=False), w0 if k == 0 else [], mm)
                p.op("tensor", lambda E, pbG=pbG, xh=xh, k=k:
                     E.matmul(pbG[:, 0:8], xh, self.gwl[:, k * 8:(k + 1) * 8],
                              start=False, stop=False), [], mm)
                t_g = p.op("tensor", lambda E, pbG=pbG, xl=xl, k=k:
                           E.matmul(pbG[:, 0:8], xl, self.gwh[:, k * 8:(k + 1) * 8],
                                    start=False, stop=(k == 7)), [], mm)
            sl = pbG[:, 0:8]
            t1 = p.op("vector", lambda E, sl=sl: E.reduce_max(gl[:, 0:1], sl, axis=AXX), [t_g], dv)
            t2 = p.op("vector", lambda E: E.tensor_scalar(gl[:, 1:2], gl[:, 0:1], -1.0, None, ALU.mult), [t1], dv)
            t3 = p.op("scalar", lambda E, sl=sl: E.activation(gl[:, 2:10], sl, AF.Exp, bias=gl[:, 1:2],
                                                              accum_out=gl[:, 10:11]), [t2], ac)
            t4 = p.op("vector", lambda E: E.reciprocal(gl[:, 11:12], gl[:, 10:11]), [t3], dv)
            t5 = p.op("vector", lambda E: E.tensor_mul(gl[:, 12:13], gl[:, 10:11], gl[:, 11:12]), [t4], dv)
            t6 = p.op("vector", lambda E: E.tensor_scalar(gl[:, 12:13], gl[:, 12:13], -1.0, 2.0, ALU.mult, ALU.add), [t5], dv)
            t7 = p.op("vector", lambda E: E.tensor_mul(gl[:, 11:12], gl[:, 11:12], gl[:, 12:13]), [t6], dv)
            t8 = p.op("vector", lambda E: E.tensor_scalar(gl[:, 13:21], gl[:, 2:10], gl[:, 11:12], None, ALU.mult), [t7], dv)
            t9 = p.op("vector", lambda E: E.tensor_add(gl[:, 21:29], gl[:, 13:21], self.gbias[:]), [t8, self.t_gb], dv)
            ta = p.op("vector", lambda E: E.reduce_max(gl[:, 29:30], gl[:, 21:29], axis=AXX), [t9], dv)
            tb = p.op("vector", lambda E: E.tensor_scalar(gl[:, 30:38], gl[:, 21:29], gl[:, 29:30], None, ALU.is_ge), [ta], dv)
            tc = p.op("vector", lambda E: E.scalar_tensor_tensor(gl[:, 21:29], gl[:, 30:38], -1e9, gl[:, 21:29],
                                                                 ALU.mult, ALU.add), [tb], dv)
            td = p.op("vector", lambda E: E.reduce_max(gl[:, 29:30], gl[:, 21:29], axis=AXX), [tc], dv)
            te_ = p.op("vector", lambda E: E.tensor_scalar(gl[:, 21:29], gl[:, 21:29], gl[:, 29:30], None, ALU.is_ge), [td], dv)
            tf = p.op("vector", lambda E: E.tensor_add(gl[:, 30:38], gl[:, 30:38], gl[:, 21:29]), [te_], dv)
            tg = p.op("vector", lambda E, tt=tt: E.tensor_mul(self.GS[:, tt * 8:(tt + 1) * 8], gl[:, 13:21], gl[:, 30:38]), [tf], dv)
            gate_done[tt % 2] = tg
        gs_tok = p.tok(dv)
        if self.debug and first_call:
            dgs = self.nc.dram_tensor("dbg_GS", [128, self.TT * NE], F32, kind="ExternalOutput").ap()
            tdg = p.dma("sync", dgs, self.GS[:], waits=[gs_tok])
            p.op("sync", lambda E: None, waits=[tdg])

        w13_free = {0: arena_gate, 1: arena_gate, 2: arena_gate}
        w2_free = {}
        hT_free = {}
        h_ps_free = {}
        tsb_free = NONE_TOK
        w2ev = {i: NONE_TOK for i in range(8)}
        experts = [("r", e) for e in range(NE)] + [("s", 0)]
        TGS = min(512, T)
        NTI = TGS // 128
        n_tg = T // TGS
        w2_rr = [0]
        for tg0 in range(0, n_tg, 2):
            jobs = [("r", e, list(range(tg0, min(tg0 + 2, n_tg)))) for e in range(NE)]
            jobs += [("s", 0, [tgx]) for tgx in range(tg0, min(tg0 + 2, n_tg))]
            for kind, e, tg_list in jobs:
                ic_n = 8 if kind == "r" else 16
                h_done = {}
                pend = None
                views = {}
                hi_tok = {}
                for ici in range(ic_n + 1):
                    if ici < ic_n:
                        parx = ici % 3
                        w13 = C[:, f["w13"] + parx * 2048: f["w13"] + parx * 2048 + 2048]
                        whi = w13[:, 0:1024].bitcast(BF16)
                        wlo = w13[:, 1024:2048].bitcast(BF16)
                        s13 = self.w13d[e, ici] if kind == "r" else self.sw13d[ici]
                        t_hi = p.dma("gpsimd", whi[:], s13, waits=[w13_free[parx]])
                        t_ng = p.op("vector", lambda E, wlo=wlo, whi=whi:
                                    E.tensor_scalar(wlo[:], whi[:], -1.0, None, ALU.mult),
                                    [t_hi, w13_free[parx]], dv)
                        views[ici] = (whi, wlo, s13, t_ng)
                        hi_tok[ici] = t_hi
                    if ici >= 1:
                        ic = ici - 1
                        par = ic % 3
                        whi, wlo, s13, t_ng = views[ic]
                        t_acc = p.dma("gpsimd", wlo[:], s13, waits=[t_ng], accum_op=ALU.add)
                        w1h = whi[:, 0:1024]
                        w3h = whi[:, 1024:2048]
                        w1l = wlo[:, 0:1024]
                        w3l = wlo[:, 1024:2048]
                        w_w = [hi_tok[ic], t_ng]
                        for tgl, tg in enumerate(tg_list):
                            tokbase = tg * TGS
                            sp = (ic * len(tg_list) + tgl) % 4
                            pb1 = self.pb[sp]
                            pb3 = self.pb[4 + sp]
                            w0 = list(xnT_tok) + w_w + [h_ps_free.get(sp, gs_tok)]
                            t_h1 = t_h3 = None
                            for k in range(8):
                                xh = self.Ah[:, k * T + tokbase: k * T + tokbase + TGS]
                                xl = self.Al[:, k * T + tokbase: k * T + tokbase + TGS]
                                p.op("tensor", lambda E, pb1=pb1, xh=xh, w1h=w1h, k=k:
                                     E.matmul(pb1[:, 0:TGS], w1h[:, k * 128:(k + 1) * 128], xh,
                                              start=(k == 0), stop=False), w0 if k == 0 else [], mm)
                                p.op("tensor", lambda E, pb1=pb1, xl=xl, w1h=w1h, k=k:
                                     E.matmul(pb1[:, 0:TGS], w1h[:, k * 128:(k + 1) * 128], xl,
                                              start=False, stop=False), [], mm)
                            for k in range(8):
                                xh = self.Ah[:, k * T + tokbase: k * T + tokbase + TGS]
                                t_h1 = p.op("tensor", lambda E, pb1=pb1, xh=xh, w1l=w1l, k=k:
                                            E.matmul(pb1[:, 0:TGS], w1l[:, k * 128:(k + 1) * 128], xh,
                                                     start=False, stop=(k == 7)), [t_acc] if k == 0 else [], mm)
                            for k in range(8):
                                xh = self.Ah[:, k * T + tokbase: k * T + tokbase + TGS]
                                xl = self.Al[:, k * T + tokbase: k * T + tokbase + TGS]
                                p.op("tensor", lambda E, pb3=pb3, xh=xh, w3h=w3h, k=k:
                                     E.matmul(pb3[:, 0:TGS], w3h[:, k * 128:(k + 1) * 128], xh,
                                              start=(k == 0), stop=False), [], mm)
                                p.op("tensor", lambda E, pb3=pb3, xl=xl, w3h=w3h, k=k:
                                     E.matmul(pb3[:, 0:TGS], w3h[:, k * 128:(k + 1) * 128], xl,
                                              start=False, stop=False), [], mm)
                            for k in range(8):
                                xh = self.Ah[:, k * T + tokbase: k * T + tokbase + TGS]
                                t_h3 = p.op("tensor", lambda E, pb3=pb3, xh=xh, w3l=w3l, k=k:
                                            E.matmul(pb3[:, 0:TGS], w3l[:, k * 128:(k + 1) * 128], xh,
                                                     start=False, stop=(k == 7)), [], mm)
                            t_si = p.op("scalar", lambda E, pb1=pb1: E.activation(tsb[:, 0:TGS], pb1[:, 0:TGS], AF.Silu),
                                        [t_h1, tsb_free], ac)
                            t_mu = p.op("vector", lambda E, pb3=pb3:
                                        E.tensor_mul(tsb[:, 0:TGS], tsb[:, 0:TGS], pb3[:, 0:TGS]), [t_si, t_h3], dv)
                            hoff = ic * len(tg_list) * TGS + tgl * TGS
                            hsh = hTh[:, hoff: hoff + TGS]
                            hsl = hTl[:, hoff: hoff + TGS]
                            t_hh = p.op("scalar", lambda E, hsh=hsh: E.copy(hsh, tsb[:, 0:TGS]),
                                        [t_mu, hT_free.get((ic, tgl), arena_gate)], ac)
                            t_h = p.op("vector", lambda E, hsl=hsl, hsh=hsh: E.tensor_sub(hsl, tsb[:, 0:TGS], hsh),
                                       [t_hh], dv)
                            tsb_free = t_h
                            h_ps_free[sp] = t_h
                            h_done[(ic, tgl)] = t_h
                        w13_free[par] = t_h
                h_all = p.tok(dv)
                for half in range(2):
                    w2v = {}
                    w2hi_tok = {}
                    for ici in range(ic_n + 1):
                        if ici < ic_n:
                            slot = w2_rr[0] % 3
                            w2_rr[0] += 1
                            w2r = C[:, f["w2"] + slot * 512: f["w2"] + slot * 512 + 512]
                            w2h = w2r[:, 0:256].bitcast(BF16)
                            w2l = w2r[:, 256:512].bitcast(BF16)
                            if kind == "r":
                                src2 = self.w2d[e, ici * 128:(ici + 1) * 128, half * 512:(half + 1) * 512]
                            else:
                                src2 = self.sw2d[ici * 128:(ici + 1) * 128, half * 512:(half + 1) * 512]
                            t_a2 = p.dma("gpsimd", w2h, src2, waits=[w2_free.get(slot, arena_gate)])
                            t_b2 = p.op("vector", lambda E, w2l=w2l, w2h=w2h:
                                        E.tensor_scalar(w2l, w2h, -1.0, None, ALU.mult),
                                        [t_a2, w2_free.get(slot, arena_gate)], dv)
                            w2v[ici] = (w2h, w2l, src2, t_b2, slot)
                            w2hi_tok[ici] = t_a2
                        if ici >= 1:
                            ic = ici - 1
                            w2h, w2l, src2, t_b2, slot = w2v[ic]
                            t_c2 = p.dma("gpsimd", w2l, src2, waits=[t_b2], accum_op=ALU.add)
                            for tgl in range(len(tg_list)):
                                for ti in range(NTI):
                                    bank = tgl * NTI + ti
                                    pbW = self.pb[bank]
                                    hoff = ic * len(tg_list) * TGS + tgl * TGS + ti * 128
                                    lh = hTh[:, hoff: hoff + 128]
                                    ll = hTl[:, hoff: hoff + 128]
                                    wde = [h_done[(ic, tgl)], w2hi_tok[ic]]
                                    if ic == 0:
                                        wde.append(w2ev[bank])
                                    p.op("tensor", lambda E, pbW=pbW, lh=lh, w2h=w2h, s=(ic == 0):
                                         E.matmul(pbW[:, 0:512], lh, w2h, start=s, stop=False), wde, mm)
                                    p.op("tensor", lambda E, pbW=pbW, ll=ll, w2h=w2h:
                                         E.matmul(pbW[:, 0:512], ll, w2h, start=False, stop=False), [], mm)
                            first_lo = True
                            for tgl in range(len(tg_list)):
                                for ti in range(NTI):
                                    bank = tgl * NTI + ti
                                    pbW = self.pb[bank]
                                    hoff = ic * len(tg_list) * TGS + tgl * TGS + ti * 128
                                    lh = hTh[:, hoff: hoff + 128]
                                    t_w = p.op("tensor", lambda E, pbW=pbW, lh=lh, w2l=w2l, st=(ic == ic_n - 1):
                                               E.matmul(pbW[:, 0:512], lh, w2l, start=False, stop=st),
                                               [t_c2] if first_lo else [], mm)
                                    first_lo = False
                            w2_free[slot] = t_w
                    for tgl, tg in enumerate(tg_list):
                        tokbase = tg * TGS
                        for ti in range(NTI):
                            bank = tgl * NTI + ti
                            tt_g = (tokbase // 128) + ti
                            pbW = self.pb[bank]
                            Bsl = self.B[:, tt_g * D + half * 512: tt_g * D + half * 512 + 512]
                            if kind == "r":
                                gcol = self.GS[:, tt_g * 8 + e: tt_g * 8 + e + 1]
                                t_ev = p.op("vector", lambda E, Bsl=Bsl, pbW=pbW, gcol=gcol:
                                            E.scalar_tensor_tensor(Bsl, pbW[:, 0:512], gcol, Bsl, ALU.mult, ALU.add),
                                            [t_w, gs_tok, self.tok_B], dv)
                            else:
                                t_ev = p.op("vector", lambda E, Bsl=Bsl, pbW=pbW:
                                            E.tensor_add(Bsl, Bsl, pbW[:, 0:512]), [t_w, self.tok_B], dv)
                            w2ev[bank] = t_ev
                mm_now = p.tok(mm)
                for ic in range(16):
                    for tg2 in range(2):
                        hT_free[(ic, tg2)] = mm_now
                last_ev = w2ev[max(k for k in w2ev if isinstance(k, int))]
                for sp in range(4):
                    h_ps_free[sp] = last_ev
        self.tok_B = p.tok(dv)
        self.gs_read = p.tok(dv)
        return p.tok(mm)

    def emit(self):
        p = self.P
        t_id = p.dma("sync", self.ident[:], self.identd[:])
        t_mq = p.dma("sync", self.maskqm[:], self.maskqmd[:])
        self.t_maskqm = t_mq
        t_on = p.op("vector", lambda E: E.memset(self.ones1[:], 1.0), [], self.dv)
        p.op("vector", lambda E: E.memset(self.ones_bf[:], 1.0), [], self.dv)
        self.t_const = p.op("vector", lambda E: E.memset(self.ck[:, 0:1], MSHIFT), [], self.dv)
        self.R_free_init = t_on
        self.gs_read = NONE_TOK
        t_x = p.dma("sync", self.B.rearrange("p (t d) -> p t d", d=D),
                    self.x_in.rearrange("(t p) d -> p t d", p=128))
        self.tok_B = t_x
        for eng in ("tensor", "vector", "scalar"):
            p.op(eng, lambda E: None, waits=[t_id, t_x, t_mq])

        self.ln_stage(extra=[t_x])
        self.dump_A("xnT1")
        g = self.fence()
        g = self.attn_stage(self.wqkv_a, self.wo_a, use_rope=True, causal=True, arena_gate=g)
        self.dump_B("x1")
        self.fence()
        self.ln_stage()
        g = self.fence()
        g = self.ffn_stage(arena_gate=g, first_call=True)
        self.dump_B("x2")
        self.fence()
        self.ln_stage()
        g = self.fence()
        g = self.attn_stage(self.wqkv_b, self.wo_b, use_rope=False, causal=False, arena_gate=g)
        self.dump_B("x3")
        self.fence()
        self.ln_stage()
        g = self.fence()
        g = self.ffn_stage(arena_gate=g, first_call=False)
        self.fence()

        t_y = p.dma("sync", self.y_out.rearrange("(t p) d -> p t d", p=128),
                    self.B.rearrange("p (t d) -> p t d", d=D), waits=[self.tok_B])
        p.op("sync", lambda E: None, waits=[t_y])
        p.build()


_BUILD_CACHE = {}


def build_nc(T=2048, detect_races=False, debug=False):
    key = (T, detect_races, debug)
    if key in _BUILD_CACHE:
        return _BUILD_CACHE[key]
    es = ExitStack()
    nc = bass.Bass("TRN2", target_bir_lowering=False, detect_race_conditions=detect_races)
    b = Builder(nc, es, T)
    b.debug = debug
    b.emit()
    es.close()
    _BUILD_CACHE[key] = nc
    return nc



def _wblk(wT):
    f32 = np.float32
    out = np.empty((NE, 8, 128, 1024), dtype=f32)
    for e in range(NE):
        for ic in range(8):
            out[e, ic] = wT[e][:, ic * 128:(ic + 1) * 128].reshape(8, 128, 128).transpose(1, 0, 2).reshape(128, 1024)
    return np.ascontiguousarray(out)


def _swblk(swT):
    f32 = np.float32
    out = np.empty((16, 128, 1024), dtype=f32)
    for ic in range(16):
        out[ic] = swT[:, ic * 128:(ic + 1) * 128].reshape(8, 128, 128).transpose(1, 0, 2).reshape(128, 1024)
    return np.ascontiguousarray(out)


def _stage_weights(T, inputs):
    f32 = np.float32
    g = lambda k: np.asarray(inputs[k], dtype=f32)

    def qkv_blocked(wqT, wkT, wvT):
        out = np.empty((8, 128, 3072), dtype=f32)
        for hp in range(8):
            blk = np.concatenate([wqT[:, hp * 128:(hp + 1) * 128],
                                  wkT[:, hp * 128:(hp + 1) * 128],
                                  wvT[:, hp * 128:(hp + 1) * 128]], axis=1)
            out[hp] = blk.reshape(8, 128, 384).transpose(1, 0, 2).reshape(128, 3072)
        return np.ascontiguousarray(out)

    wqkv_a = qkv_blocked(g("L0_attn_a_Wq").T, g("L0_attn_a_Wk").T, g("L0_attn_a_Wv").T)
    wo_a = np.ascontiguousarray(g("L0_attn_a_Wo").T)
    wqkvT = g("L0_attn_b_w_qkv").T
    wqkv_b = qkv_blocked(wqkvT[:, 0:1024], wqkvT[:, 1024:2048], wqkvT[:, 2048:3072])
    wo_b = np.ascontiguousarray(g("L0_attn_b_w_proj").T)

    pos = np.arange(T, dtype=np.float64)
    inv = 1.0 / (10000.0 ** (np.arange(0, DK, 2, dtype=np.float64) / DK))
    ang = pos[:, None] * inv[None, :]
    cos32 = np.cos(ang).astype(f32)
    sin32 = np.sin(ang).astype(f32)
    cosx = np.empty((T, 128), dtype=f32)
    sinx = np.empty((T, 128), dtype=f32)
    for hl in range(2):
        cosx[:, hl * 64 + 0::2][:, :32] = cos32
        cosx[:, hl * 64 + 1::2][:, :32] = cos32
        sinx[:, hl * 64 + 0::2][:, :32] = sin32
        sinx[:, hl * 64 + 1::2][:, :32] = sin32

    QGS = min(512, T)
    DQG = QGS // 128
    maskd = np.zeros((128, DQG * QGS), dtype=f32)
    for j in range(DQG):
        kloc = j * 128 + np.arange(128)[:, None]
        qloc = np.arange(QGS)[None, :]
        maskd[:, j * QGS:(j + 1) * QGS] = np.where(kloc > qloc, NEGBIG, 0.0)

    return {
        "wqkv_a": wqkv_a, "wo_a": wo_a, "wqkv_b": wqkv_b, "wo_b": wo_b,
        "cosx": cosx, "sinx": sinx, "maskd": maskd,
        "maskqmd": np.where(np.arange(128)[None, :] > np.arange(128)[:, None], NEGBIG, 0.0).astype(f32),
        "identd": np.eye(128, dtype=f32),
        "gwd": np.ascontiguousarray(g("L0_ffn_gate_w").T.reshape(8, 128, NE).transpose(1, 0, 2).reshape(128, 64)),
        "gbiasd": np.ascontiguousarray(np.broadcast_to(g("L0_ffn_gate_bias"), (128, NE)).copy()),
        "w13d": np.ascontiguousarray(np.concatenate(
            [_wblk(np.transpose(g("L0_ffn_w1"), (0, 2, 1))),
             _wblk(np.transpose(g("L0_ffn_w3"), (0, 2, 1)))], axis=3)),
        "w2d": np.ascontiguousarray(np.transpose(g("L0_ffn_w2"), (0, 2, 1))),
        "sw13d": np.ascontiguousarray(np.concatenate(
            [_swblk(g("L0_ffn_sw1").T), _swblk(g("L0_ffn_sw3").T)], axis=2)),
        "sw2d": np.ascontiguousarray(g("L0_ffn_sw2").T),
    }


def kernel(**inputs):
    x = np.asarray(inputs["L0_attn_a_x"], dtype=np.float32)
    B_, T, D_ = x.shape
    staged = _stage_weights(T, inputs)
    nc = build_nc(T=T, detect_races=False)
    in_maps = []
    for b in range(B_):
        m = dict(staged)
        m["x_in"] = np.ascontiguousarray(x[b])
        in_maps.append(m)
    res = run_bass_kernel_spmd(nc, in_maps, core_ids=list(range(B_)))
    out = np.stack([np.asarray(r["y_out"]) for r in res.results], axis=0)
    return out.astype(np.float32)
